# version 34
# baseline (speedup 1.0000x reference)
import numpy as np
import ml_dtypes
from contextlib import ExitStack
import concourse.bass as bass
import concourse.mybir as mybir
from concourse.bass_utils import run_bass_kernel_spmd

F32 = mybir.dt.float32
BF16 = mybir.dt.bfloat16
AF = mybir.ActivationFunctionType
ALU = mybir.AluOpType
AX = mybir.AxisListType

T = 4096
Dm = 1024
NJ = 8
NT = 32
E8 = 8
FF = 4096
NF = 32
HP = 8
TTB = 1024
NTT = 4
CCAP = 1536
ESTRIDE = 2048
DUMPROW = 8 * ESTRIDE
IDXROWS = 16512

ENGINES = ("tensor", "scalar", "vector", "sync", "gpsimd")
DMA_FAMS = ("fb", "fx", "fxo", "fxt", "fs", "fc1", "fc2", "fcx", "fcy", "fas", "fxn",
            "fw1", "fw2", "fat", "fo", "fix", "fiv", "fga", "fys", "fyw", "fy")
SEMS = ("spe", "sse", "sve", "ssp", "spo") + DMA_FAMS
ESEM = {"tensor": "spe", "scalar": "sse", "vector": "sve", "sync": "ssp", "gpsimd": "spo"}


class KB:

    def __init__(self, nc, block, sems):
        self.nc, self.block, self.sem = nc, block, sems
        self.cnt = {s: 0 for s in SEMS}
        self.hw = {e: {s: 0 for s in SEMS} for e in ENGINES}
        self.bank_free = {b: None for b in range(8)}
        self.q = {e: [] for e in ENGINES}
        self.trace = {e: [] for e in ENGINES}

    def wait(self, eng, sem, val):
        if val is None or val <= 0 or self.hw[eng][sem] >= val:
            return
        self.hw[eng][sem] = val
        h = self.sem[sem]
        self.q[eng].append(lambda E, _h=h, _v=val: E.wait_ge(_h, _v))
        self.trace[eng].append((((sem, val),), ()))

    def waitp(self, eng, pair):
        if pair is not None:
            self.wait(eng, pair[0], pair[1])

    def do(self, eng, fn, inc=None, inc_by=1):
        if inc is None:
            self.q[eng].append(fn)
            self.trace[eng].append(((), ()))
            return None
        h = self.sem[inc]
        self.q[eng].append(lambda E, _fn=fn, _h=h, _k=inc_by: _fn(E).then_inc(_h, _k))
        self.cnt[inc] += inc_by
        v = self.cnt[inc]
        self.trace[eng].append(((), ((inc, v),)))
        self.hw[eng][inc] = max(self.hw[eng][inc], v)
        return v

    def dma(self, eng, out_ap, in_ap, fam="fb"):
        h = self.sem[fam]
        self.q[eng].append(lambda E, _o=out_ap, _i=in_ap, _h=h:
                           E.dma_start(_o, _i).then_inc(_h, 16))
        self.cnt[fam] += 16
        v = self.cnt[fam]
        self.trace[eng].append(((), ((fam, v),)))
        return (fam, v)

    def drain(self, eng):
        self.q[eng].append(lambda E: E.drain())
        self.trace[eng].append(((), ()))

    def dma_indirect_out(self, out_ap, idx_ap, in_ap, fam, op=None):
        h = self.sem[fam]
        alu = op if op is not None else ALU.bypass
        self.q["gpsimd"].append(
            lambda E, _o=out_ap, _x=idx_ap, _i=in_ap, _h=h, _a=alu:
            E.indirect_dma_start(_o, bass.IndirectOffsetOnAxis(_x, 0), _i, None,
                                 oob_is_err=False, compute_op=_a).then_inc(_h, 16))
        self.cnt[fam] += 16
        v = self.cnt[fam]
        self.trace["gpsimd"].append(((), ((fam, v),)))
        return (fam, v)

    def dma_indirect_in(self, out_ap, in_ap, idx_ap, fam):
        h = self.sem[fam]
        self.q["gpsimd"].append(
            lambda E, _o=out_ap, _x=idx_ap, _i=in_ap, _h=h:
            E.indirect_dma_start(_o, None, _i, bass.IndirectOffsetOnAxis(_x, 0),
                                 oob_is_err=False).then_inc(_h, 16))
        self.cnt[fam] += 16
        v = self.cnt[fam]
        self.trace["gpsimd"].append(((), ((fam, v),)))
        return (fam, v)

    def do_drained(self, eng, fn, inc):
        self.do(eng, fn)
        return self.do(eng, lambda E: E.drain(), inc=inc)

    def use_bank(self, eng, b):
        self.waitp(eng, self.bank_free[b])

    def release_bank(self, b, sem, val):
        self.bank_free[b] = (sem, val)

    def barrier(self):
        marks = []
        for e in ENGINES:
            s = ESEM[e]
            if e == "tensor":
                marks.append((s, self.cnt[s]))
            else:
                marks.append((s, self.do(e, lambda E: E.nop(), inc=s)))
        targets = marks + [(f, self.cnt[f]) for f in DMA_FAMS]
        for e in ENGINES:
            for s, v in targets:
                self.wait(e, s, v)

    def flush(self):
        for e in ENGINES:
            items = self.q[e]
            if not items:
                continue
            fn = getattr(self.block, {"tensor": "tensor", "scalar": "scalar",
                                      "vector": "vector", "sync": "sync",
                                      "gpsimd": "gpsimd"}[e])

            def run(E, _items=items):
                for f in _items:
                    f(E)
            fn(run)
            self.q[e] = []

    def simulate(self):
        ptr = {e: 0 for e in ENGINES}
        val = {s: 0 for s in SEMS}
        progress = True
        while progress:
            progress = False
            for e in ENGINES:
                while ptr[e] < len(self.trace[e]):
                    waits, incs = self.trace[e][ptr[e]]
                    if any(val[s] < v for s, v in waits):
                        break
                    for s, v in incs:
                        val[s] = max(val[s], v)
                    ptr[e] += 1
                    progress = True
        stuck = {e: (ptr[e], len(self.trace[e]), self.trace[e][ptr[e]][0])
                 for e in ENGINES if ptr[e] < len(self.trace[e])}
        if stuck:
            raise RuntimeError(f"build-time deadlock: {stuck} semvals={val}")

    def final_wait(self):
        for f in DMA_FAMS:
            self.hw["gpsimd"][f] = 0
            self.wait("gpsimd", f, self.cnt[f])


def build_program(upto="DE", debug=False, b_experts=E8, routed=False):
    nc = bass.Bass("TRN2", target_bir_lowering=False)

    x_h = nc.dram_tensor("x", [T, Dm], F32, kind="ExternalInput").ap()
    wq_h = nc.dram_tensor("wq", [Dm, Dm], F32, kind="ExternalInput").ap()
    wk_h = nc.dram_tensor("wk", [Dm, Dm], F32, kind="ExternalInput").ap()
    wv_h = nc.dram_tensor("wv", [Dm, Dm], F32, kind="ExternalInput").ap()
    wo_h = nc.dram_tensor("wo", [Dm, Dm], F32, kind="ExternalInput").ap()
    bq_h = nc.dram_tensor("bq", [8, 128], F32, kind="ExternalInput").ap()
    bk_h = nc.dram_tensor("bk", [8, 128], F32, kind="ExternalInput").ap()
    bv_h = nc.dram_tensor("bv", [1, Dm], F32, kind="ExternalInput").ap()
    bo_h = nc.dram_tensor("bo", [1, Dm], F32, kind="ExternalInput").ap()
    wg_h = nc.dram_tensor("wg", [Dm, 8], F32, kind="ExternalInput").ap()
    w1_h = nc.dram_tensor("w1", [E8, Dm, FF], F32, kind="ExternalInput").ap()
    b1_h = nc.dram_tensor("b1", [E8, FF], F32, kind="ExternalInput").ap()
    w2_h = nc.dram_tensor("w2", [E8, FF, Dm], F32, kind="ExternalInput").ap()
    b2_h = nc.dram_tensor("b2", [E8, Dm], F32, kind="ExternalInput").ap()
    out_h = nc.dram_tensor("out", [T, Dm], F32, kind="ExternalOutput").ap()

    skind = "ExternalOutput" if debug else "Internal"
    xnt_h = nc.dram_tensor("dbg_xnt" if debug else "xnt", [Dm, T], BF16, kind=skind).ap()
    xat_h = nc.dram_tensor("dbg_xat" if debug else "xat", [Dm, T], BF16, kind=skind).ap()
    w1b_h = nc.dram_tensor("w1b", [E8, NF, 128, NJ, 128], BF16, kind="Internal").ap()
    xtok_h = nc.dram_tensor("xtok", [T, Dm], F32, kind="Internal").ap()
    idxP_h = nc.dram_tensor("idxP", [16640, 2], F32, kind="Internal").ap()
    y_h = nc.dram_tensor("dbg_yh" if debug else "yh", [T, Dm], F32,
                         kind="ExternalOutput" if debug else "Internal").ap()
    w2b_h = nc.dram_tensor("w2b", [E8, 2, NF, 128, 512], BF16, kind="Internal").ap()
    if debug:
        dbg_pn = nc.dram_tensor("dbg_pn", [128, 256], F32, kind="ExternalOutput").ap()
        dbg_y = nc.dram_tensor("dbg_y", [128, NT * Dm], BF16, kind="ExternalOutput").ap()
        dbg_s = nc.dram_tensor("dbg_s", [T, Dm], BF16, kind="ExternalOutput").ap()
        dbg_h = nc.dram_tensor("dbg_h", [128, NF * TTB], BF16, kind="ExternalOutput").ap()

    st = ExitStack()
    sems = {s: st.enter_context(nc.semaphore(s)) for s in SEMS}
    sb = lambda name, shape, dt: st.enter_context(nc.sbuf_tensor(name, shape, dt))

    ident = sb("ident", [128, 128], F32)
    ident1 = sb("ident1", [1, 1], F32)
    ones1 = sb("ones1", [1, 128], F32)
    wg_bf = sb("wg_bf", [128, 64], BF16)
    thrq = sb("thrq", [128, 8], F32)
    thrk = sb("thrk", [128, 8], F32)
    bvb = sb("bvb", [128, Dm], F32)
    thro = sb("thro", [128, Dm], F32)
    b1c = sb("b1c", [128, 256], F32)
    b2bf = sb("b2bf", [8, Dm], BF16)
    Pn = sb("Pn", [128, 256], F32)
    PnT = sb("PnT", [8, T], BF16)
    ps = st.enter_context(nc.psum_tensor("ps", [128, 4096], F32))
    ps3 = ps[:, :].rearrange("p (g c) -> p g c", c=128)

    block = st.enter_context(nc.Block())
    K = KB(nc, block, sems)

    init_st = ExitStack()
    isb = lambda n, sh, dt: init_st.enter_context(nc.sbuf_tensor(n, sh, dt))
    bqn = isb("bqn", [8, 128], F32)
    bkn = isb("bkn", [8, 128], F32)
    bvr = isb("bvr", [1, Dm], F32)
    bor = isb("bor", [1, Dm], F32)
    b1n = isb("b1n", [8, FF], F32)
    b2f = isb("b2f", [8, Dm], F32)
    wgf = isb("wgf", [128, 64], F32)

    K.do("gpsimd", lambda E: E.memset(ident[:], 0.0))
    K.drain("gpsimd")
    K.do("gpsimd", lambda E: E.affine_select(out=ident[:], in_=ident[:],
                                             compare_op=ALU.not_equal, fill=1.0,
                                             base=0, pattern=[[-1, 128]],
                                             channel_multiplier=1))
    K.do("gpsimd", lambda E: E.memset(ident1[:], 1.0))
    po_ready = K.do("gpsimd", lambda E: E.memset(ones1[:], 1.0), inc="spo")

    K.dma("sync", bkn[:], bk_h)
    K.dma("sync", bqn[:], bq_h)
    K.dma("sync", bor[:], bo_h)
    K.dma("sync", bvr[:], bv_h)
    K.dma("sync", b1n[:], b1_h)
    K.dma("sync", b2f[:], b2_h)
    for c in range(8):
        K.dma("sync", wgf[:, 8 * c:8 * c + 8], wg_h[128 * c:128 * c + 128, :])
    d_init = ("fb", K.cnt["fb"])

    K.waitp("vector", d_init)
    v_wg = K.do("vector", lambda E: E.tensor_copy(wg_bf[:], wgf[:]), inc="sve")
    v_b2 = K.do("vector", lambda E: E.tensor_copy(b2bf[:], b2f[:]), inc="sve")
    v_bo = K.do("vector", lambda E: E.tensor_scalar(bor[:], bor[:], 1.0, -1.0,
                                                    ALU.subtract, ALU.mult), inc="sve")

    K.wait("tensor", "spo", po_ready)
    K.waitp("tensor", d_init)
    K.do("tensor", lambda E: E.transpose(ps[:, 0:8], bqn[:], ident[0:8, 0:8]))
    p_bk = K.do("tensor", lambda E: E.transpose(ps[:, 8:16], bkn[:], ident[0:8, 0:8]),
                inc="spe")
    K.waitp("tensor", d_init)
    p_b1 = None
    for fc in range(NF):
        p_b1 = K.do("tensor", lambda E, _fc=fc: E.transpose(
            ps[:, 512 + 8 * _fc:520 + 8 * _fc],
            b1n[0:8, 128 * _fc:128 * _fc + 128], ident[0:8, 0:8]),
            inc="spe" if fc == NF - 1 else None)
    K.waitp("tensor", d_init)
    K.do("tensor", lambda E: E.matmul(ps[:, 1024:1536], ones1[:], bvr[0:1, 0:512],
                                      start=True, stop=True))
    p_bv = K.do("tensor", lambda E: E.matmul(ps[:, 1536:2048], ones1[:], bvr[0:1, 512:1024],
                                             start=True, stop=True), inc="spe")
    K.wait("tensor", "sve", v_bo)
    K.do("tensor", lambda E: E.matmul(ps[:, 2048:2560], ones1[:], bor[0:1, 0:512],
                                      start=True, stop=True))
    p_bo = K.do("tensor", lambda E: E.matmul(ps[:, 2560:3072], ones1[:], bor[0:1, 512:1024],
                                             start=True, stop=True), inc="spe")

    K.wait("vector", "spe", p_bk)
    K.do("vector", lambda E: E.tensor_scalar(thrq[:], ps[:, 0:8], 1.0, -1.0,
                                             ALU.subtract, ALU.mult))
    K.do("vector", lambda E: E.tensor_scalar(thrk[:], ps[:, 8:16], 1.0, -1.0,
                                             ALU.subtract, ALU.mult))
    K.wait("vector", "spe", p_b1)
    K.do("vector", lambda E: E.tensor_copy(b1c[:], ps[:, 512:768]))
    K.wait("vector", "spe", p_bv)
    K.do("vector", lambda E: E.tensor_copy(bvb[:], ps[:, 1024:2048]))
    K.wait("vector", "spe", p_bo)
    v_init = K.do("vector", lambda E: E.tensor_copy(thro[:], ps[:, 2048:3072]), inc="sve")
    for b in range(6):
        K.release_bank(b, "sve", v_init)
    K.flush()
    init_st.close()

    r2_st = ExitStack()
    r2_sb = lambda n, sh, dt: r2_st.enter_context(nc.sbuf_tensor(n, sh, dt))
    if routed:
        off32 = r2_sb("off32", [128, 256], mybir.dt.int32)
        tok32 = r2_sb("tok32", [128, 32], F32)
        vals = r2_sb("vals", [128, 512], F32)
        zero_sb = r2_sb("zero_sb", [128, 260], F32)
    xnT_st = ExitStack()
    xnT = xnT_st.enter_context(nc.sbuf_tensor("xnT", [128, NJ * T], BF16))
    xnT3 = xnT[:, :].rearrange("p (j n) -> p j n", n=T)

    a_st = ExitStack()
    asb = lambda n, sh, dt: a_st.enter_context(nc.sbuf_tensor(n, sh, dt))
    x_sb = [asb(f"x_sb{i}", [128, Dm], F32) for i in range(3)]
    xnf = [asb(f"xnf{i}", [128, Dm], F32) for i in range(3)]
    sq_scr = asb("sq_scr", [128, Dm], F32)
    cols = asb("cols", [128, 24], F32)

    x_free = [None, None, None]
    xnf_free = [None, None, None]
    evac_v = None
    for t in range(NT):
        sl = t % 3
        if x_free[sl] is not None:
            for pr in x_free[sl]:
                K.waitp("sync", pr)
        d_x = K.dma("sync", x_sb[sl][:], x_h[128 * t:128 * t + 128, :], fam="fx")
        co = 8 * sl
        c_xsum, c_sumsq = cols[:, co:co + 1], cols[:, co + 1:co + 2]
        c_mean, c_ex2 = cols[:, co + 2:co + 3], cols[:, co + 3:co + 4]
        c_var, c_std = cols[:, co + 4:co + 5], cols[:, co + 5:co + 6]
        c_rstd, c_nmean = cols[:, co + 6:co + 7], cols[:, co + 7:co + 8]

        K.waitp("vector", d_x)
        K.do("vector", lambda E, a=c_xsum, b=x_sb[sl]: E.reduce_sum(a, b[:], axis=AX.X))
        K.waitp("scalar", d_x)
        s_sq = K.do_drained("scalar", lambda E, b=x_sb[sl], a=c_sumsq: E.activation(
            sq_scr[:], b[:], AF.Square, accum_out=a), inc="sse")
        K.drain("vector")
        K.do("vector", lambda E, a=c_mean, b=c_xsum: E.tensor_scalar(
            a, b, 1.0 / Dm, None, ALU.mult))
        K.do("vector", lambda E, a=c_nmean, b=c_xsum: E.tensor_scalar(
            a, b, -1.0 / Dm, None, ALU.mult))
        K.wait("vector", "sse", s_sq)
        K.do("vector", lambda E, a=c_ex2, b=c_sumsq: E.tensor_scalar(
            a, b, 1.0 / Dm, 1e-5, ALU.mult, ALU.add))
        K.drain("vector")
        v_var = K.do_drained("vector", lambda E, nm=c_nmean, m=c_mean, a=c_ex2, v=c_var:
                     E.scalar_tensor_tensor(v, nm, m, a, op0=ALU.mult, op1=ALU.add),
                     inc="sve")
        K.wait("scalar", "sve", v_var)
        s_std = K.do_drained("scalar", lambda E, v=c_var, s_=c_std: E.activation(
            s_, v, AF.Sqrt), inc="sse")
        K.wait("vector", "sse", s_std)
        K.do("vector", lambda E, s_=c_std, r=c_rstd: E.reciprocal(r, s_))
        if xnf_free[sl] is not None:
            for pr in xnf_free[sl]:
                K.waitp("vector", pr)
        K.drain("vector")
        v_xn = K.do_drained("vector", lambda E, xx=x_sb[sl], m=c_mean, r=c_rstd, o=xnf[sl]:
                    E.tensor_scalar(o[:], xx[:], m, r, ALU.subtract, ALU.mult), inc="sve")
        x_free[sl] = (("sve", v_xn), ("sse", s_sq))
        if routed:
            K.wait("gpsimd", "sve", v_xn)
            d_xt = K.dma("gpsimd", xtok_h[128 * t:128 * t + 128, :], xnf[sl][:], fam="fxt")
        else:
            d_xt = None

        K.wait("tensor", "sve", v_xn)
        for b in (2 * sl, 2 * sl + 1):
            K.use_bank("tensor", b)
        p_tr = None
        for j in range(NJ):
            p_tr = K.do("tensor", lambda E, _j=j, _sl=sl, _x=xnf[sl]: E.transpose(
                ps[:, 1024 * _sl + 128 * _j:1024 * _sl + 128 * _j + 128],
                _x[:, 128 * _j:128 * _j + 128], ident[:]),
                inc="spe" if j == NJ - 1 else None)
        xnf_free[sl] = (("spe", p_tr), d_xt) if d_xt else (("spe", p_tr),)
        K.wait("vector", "spe", p_tr)
        evac_v = K.do("vector", lambda E, _t=t, _sl=sl: E.tensor_copy(
            xnT3[:, :, 128 * _t:128 * _t + 128], ps3[:, 8 * _sl:8 * _sl + 8, :]), inc="sve")
        for b in (2 * sl, 2 * sl + 1):
            K.release_bank(b, "sve", evac_v)

    evac_v = K.do("vector", lambda E: E.drain(), inc="sve")
    xnt_wr = None
    K.wait("gpsimd", "sve", evac_v)
    for j in range(NJ):
        xnt_wr = K.dma("gpsimd", xnt_h[128 * j:128 * j + 128, :], xnT[:, T * j:T * j + T], fam="fxo")
    K.flush()
    a_st.close()

    r_st = ExitStack()
    PnTf = r_st.enter_context(nc.sbuf_tensor("PnTf", [8, T], F32)) if routed else None
    a2_st = ExitStack()
    Eex = [a2_st.enter_context(nc.sbuf_tensor(f"Eex{i}", [128, 8], F32)) for i in range(2)]
    g8 = [a2_st.enter_context(nc.sbuf_tensor(f"g8_{i}", [128, 8], F32)) for i in range(2)]
    gc = a2_st.enter_context(nc.sbuf_tensor("gc", [128, 8], F32))

    K.wait("tensor", "sve", evac_v)
    K.wait("tensor", "sve", v_wg)
    eex_free = [None, None]
    pn_last = None
    for t in range(NT):
        sl = t % 2
        b_g = 4 + sl
        K.use_bank("tensor", b_g)
        p_g = None
        for j in range(NJ):
            p_g = K.do("tensor", lambda E, _j=j, _b=b_g, _t=t: E.matmul(
                ps[:, 512 * _b:512 * _b + 8],
                xnT3[:, _j, 128 * _t:128 * _t + 128],
                wg_bf[:, 8 * _j:8 * _j + 8],
                start=(_j == 0), stop=(_j == NJ - 1)),
                inc="spe" if j == NJ - 1 else None)
        co = 4 * sl
        c_m1, c_m2 = gc[:, co:co + 1], gc[:, co + 1:co + 2]
        c_den, c_s = gc[:, co + 2:co + 3], gc[:, co + 3:co + 4]
        K.wait("vector", "spe", p_g)
        v_m1 = K.do_drained("vector", lambda E, _b=b_g, m=c_m1: E.reduce_max(
            m, ps[:, 512 * _b:512 * _b + 8], axis=AX.X, negate=True), inc="sve")
        K.wait("scalar", "sve", v_m1)
        K.waitp("scalar", eex_free[sl])
        s_e = K.do_drained("scalar", lambda E, _b=b_g, m=c_m1, o=Eex[sl]: E.activation(
            o[:], ps[:, 512 * _b:512 * _b + 8], AF.Exp, bias=m), inc="sse")
        K.release_bank(b_g, "sse", s_e)
        ex, g = Eex[sl], g8[sl]
        K.wait("vector", "sse", s_e)
        K.do("vector", lambda E, ex=ex, g=g: E.tensor_scalar(
            g[:], ex[:], 1.0, -1e30, ALU.is_ge, ALU.mult))
        K.do("vector", lambda E, ex=ex, s_=c_s: E.reduce_sum(s_, ex[:], axis=AX.X))
        K.drain("vector")
        K.do("vector", lambda E, ex=ex, g=g: E.tensor_tensor(g[:], g[:], ex[:], ALU.add))
        K.drain("vector")
        K.do("vector", lambda E, g=g, m=c_m2: E.reduce_max(m, g[:], axis=AX.X))
        K.drain("vector")
        K.do("vector", lambda E, ex=ex, g=g, m=c_m2: E.tensor_scalar(
            g[:], ex[:], m, None, ALU.is_ge))
        K.drain("vector")
        K.do("vector", lambda E, ex=ex, g=g: E.tensor_tensor(g[:], g[:], ex[:], ALU.mult))
        K.drain("vector")
        K.do("vector", lambda E, g=g, d=c_den: E.reduce_sum(d, g[:], axis=AX.X))
        K.drain("vector")
        K.do("vector", lambda E, s_=c_s, d=c_den: E.scalar_tensor_tensor(
            d, s_, 1e-9, d, op0=ALU.mult, op1=ALU.add))
        K.drain("vector")
        K.do("vector", lambda E, d=c_den: E.reciprocal(d, d))
        K.drain("vector")
        v_pn = K.do_drained("vector", lambda E, g=g, d=c_den, _t=t: E.tensor_scalar(
            Pn[:, 8 * _t:8 * _t + 8], g[:], d, None, ALU.mult), inc="sve")
        eex_free[sl] = ("sve", v_pn)
        b_t = 6 + sl
        K.use_bank("tensor", b_t)
        K.wait("tensor", "sve", v_pn)
        p_pt = K.do("tensor", lambda E, _b=b_t, _t=t: E.transpose(
            ps[0:8, 512 * _b:512 * _b + 128], Pn[:, 8 * _t:8 * _t + 8], ident[:]), inc="spe")
        K.wait("vector", "spe", p_pt)
        if routed:
            K.do("vector", lambda E, _b=b_t, _t=t: E.tensor_copy(
                PnT[0:8, 128 * _t:128 * _t + 128], ps[0:8, 512 * _b:512 * _b + 128]))
            pn_last = K.do("vector", lambda E, _b=b_t, _t=t: E.tensor_copy(
                PnTf[0:8, 128 * _t:128 * _t + 128], ps[0:8, 512 * _b:512 * _b + 128]),
                inc="sve")
        else:
            pn_last = K.do("vector", lambda E, _b=b_t, _t=t: E.tensor_copy(
                PnT[0:8, 128 * _t:128 * _t + 128], ps[0:8, 512 * _b:512 * _b + 128]),
                inc="sve")
        K.release_bank(b_t, "sve", pn_last)
    if debug:
        K.wait("gpsimd", "sve", pn_last)
        K.dma("gpsimd", dbg_pn, Pn[:], fam="fo")
    K.flush()
    a2_st.close()

    if upto == "A2":
        K.final_wait(); K.flush(); K.simulate()
        r_st.close(); r2_st.close(); xnT_st.close(); st.close()
        return nc

    fix_total = 0
    if routed:
        rsb = lambda n, sh, dt: r_st.enter_context(nc.sbuf_tensor(n, sh, dt))
        maskT = rsb("maskT", [8, T], F32)
        posT = rsb("posT", [8, T], F32)
        offT = rsb("offT", [8, T], F32)
        vldT = rsb("vldT", [8, T], F32)
        eoff = rsb("eoff", [8, T], F32)

        po_r = K.do("gpsimd", lambda E: E.iota(
            eoff[:], [[0, T]], channel_multiplier=ESTRIDE,
            allow_small_or_imprecise_dtypes=True), inc="spo")
        K.do("gpsimd", lambda E: E.memset(zero_sb[:], 0.0))
        po_r = K.do("gpsimd", lambda E: E.iota(
            tok32[:], [[128, 32]], channel_multiplier=1,
            allow_small_or_imprecise_dtypes=True), inc="spo")
        d_z1 = K.dma("gpsimd", idxP_h.rearrange("(a b) c -> a (b c)", b=130),
                     zero_sb[:], fam="fix")

        K.wait("vector", "sve", pn_last)
        K.drain("vector")
        K.do("vector", lambda E: E.tensor_scalar(maskT[:], PnTf[:], 0.0, None, ALU.is_gt))
        K.drain("vector")
        K.do("vector", lambda E: E.tensor_tensor_scan(
            posT[:], maskT[:], maskT[:], 0.0, op0=ALU.add, op1=ALU.bypass))
        K.drain("vector")
        K.do("vector", lambda E: E.tensor_tensor(posT[:], posT[:], maskT[:], ALU.subtract))
        K.do("vector", lambda E: E.tensor_scalar(vldT[:], posT[:], float(CCAP), None,
                                                 ALU.is_lt))
        K.drain("vector")
        K.do("vector", lambda E: E.tensor_tensor(maskT[:], maskT[:], vldT[:], ALU.mult))
        K.wait("vector", "spo", po_r)
        K.do("vector", lambda E: E.tensor_tensor(offT[:], posT[:], eoff[:], ALU.add))
        K.drain("vector")
        K.do("vector", lambda E: E.tensor_scalar(offT[:], offT[:], -float(DUMPROW), None,
                                                 ALU.add))
        K.drain("vector")
        K.do("vector", lambda E: E.tensor_tensor(offT[:], offT[:], maskT[:], ALU.mult))
        K.drain("vector")
        v_off = K.do_drained("vector", lambda E: E.tensor_scalar(
            offT[:], offT[:], float(DUMPROW), None, ALU.add), inc="sve")

        for c in range(NT):
            for e in range(E8):
                k = 8 * c + e
                K.do("vector", lambda E, _c=c, _k=k: E.tensor_copy(
                    vals[:, 2 * _k:2 * _k + 1], tok32[:, _c:_c + 1]))
                K.do("vector", lambda E, _k=k: E.tensor_copy(
                    vals[:, 2 * _k + 1:2 * _k + 2], Pn[:, _k:_k + 1]))
        K.drain("vector")

        K.wait("tensor", "sve", v_off)
        v_o32 = None
        for c in range(NT):
            b = c % 2
            K.use_bank("tensor", b)
            p_ot = K.do("tensor", lambda E, _b=b, _c=c: E.transpose(
                ps[:, 512 * _b:512 * _b + 8], offT[0:8, 128 * _c:128 * _c + 128],
                ident[0:8, 0:8]), inc="spe")
            K.wait("vector", "spe", p_ot)
            v_o32 = K.do("vector", lambda E, _b=b, _c=c: E.tensor_copy(
                off32[:, 8 * _c:8 * _c + 8], ps[:, 512 * _b:512 * _b + 8]), inc="sve")
            K.release_bank(b, "sve", v_o32)
        v_o32 = K.do("vector", lambda E: E.drain(), inc="sve")

        K.wait("gpsimd", "sve", v_o32)
        K.waitp("gpsimd", d_z1)
        for c in range(NT):
            for e in range(E8):
                k = 8 * c + e
                K.dma_indirect_out(idxP_h, off32[:, k:k + 1],
                                   vals[:, 2 * k:2 * k + 2], fam="fix")
        fix_total = K.cnt["fix"]
    K.flush()
    r_st.close()

    c_st = ExitStack()
    csb = lambda n, sh, dt: c_st.enter_context(nc.sbuf_tensor(n, sh, dt))
    wqT = csb("wqT", [128, NJ * Dm], BF16)
    wkT = csb("wkT", [128, NJ * Dm], BF16)
    wvT = csb("wvT", [128, NJ * Dm], BF16)
    wqT3 = wqT[:, :].rearrange("p (j n) -> p j n", n=Dm)
    wkT3 = wkT[:, :].rearrange("p (j n) -> p j n", n=Dm)
    wvT3 = wvT[:, :].rearrange("p (j n) -> p j n", n=Dm)
    Qs = csb("Qs", [128, T], BF16)
    Ks = csb("Ks", [128, T], BF16)
    Vt = csb("Vt", [128, T], BF16)
    rA = csb("rA", [128, 4 * 1024], BF16)
    a_stage = csb("a_stage", [128, T], BF16)
    stg_c = [csb(f"stg_c{i}", [128, 2048], F32) for i in range(3)]
    cvo = [csb(f"cvo{i}", [128, 2048], BF16) for i in range(2)]

    conv_steps = []
    for e in range(E8):
        for k in range(NJ):
            for half in range(2):
                conv_steps.append(("w1",
                                   w1_h[e, 128 * k:128 * k + 128,
                                        2048 * half:2048 * half + 2048],
                                   w1b_h[e, 16 * half:16 * half + 16, :, k, :]
                                   .rearrange("fc p f -> p fc f")))
        for fi in range(NF):
            conv_steps.append(("w2", w2_h[e, 128 * fi:128 * fi + 128, :],
                               (w2b_h[e, 0, fi], w2b_h[e, 1, fi])))
    conv_i = [0]
    stg_ctr = [0]
    cstg_free = [None, None, None]
    conv_free = [None, None, None]
    cvo_free = [None, None]

    def emit_conv(n):
        for _ in range(n):
            i = conv_i[0]
            if i >= len(conv_steps):
                return
            conv_i[0] += 1
            kind, src_, dst = conv_steps[i]
            sl = stg_ctr[0] % 3
            stg_ctr[0] += 1
            sl2 = i % 2
            nfree = 2048 if kind == "w1" else 1024
            K.waitp("sync", cstg_free[sl])
            K.waitp("sync", conv_free[sl])
            d_in = K.dma("sync", stg_c[sl][:, 0:nfree], src_,
                         fam="fc1" if kind == "w1" else "fc2")
            K.waitp("gpsimd", d_in)
            K.waitp("gpsimd", cvo_free[sl2])
            v_c = K.do("gpsimd", lambda E, _sl=sl, _s2=sl2, _n=nfree:
                       E.tensor_copy(cvo[_s2][:, 0:_n], stg_c[_sl][:, 0:_n]),
                       inc="spo")
            conv_free[sl] = ("spo", v_c)
            cstg_free[sl] = ("spo", v_c)
            if kind == "w1":
                w_ = K.dma("gpsimd", dst,
                           cvo[sl2][:, 0:2048].rearrange("p (fc f) -> p fc f", f=128),
                           fam="fcx")
            else:
                K.dma("gpsimd", dst[0], cvo[sl2][:, 0:512], fam="fcy")
                w_ = K.dma("gpsimd", dst[1], cvo[sl2][:, 512:1024], fam="fcy")
            cvo_free[sl2] = w_

    emit_conv(48)

    cprep_done = None
    for m, (w_src, wT3m) in enumerate([(wq_h, wqT3), (wk_h, wkT3), (wv_h, wvT3)]):
        for r in range(NJ):
            sl = stg_ctr[0] % 3
            stg_ctr[0] += 1
            psl = (m * NJ + r) % 2
            K.waitp("sync", cstg_free[sl])
            d_w = K.dma("sync", stg_c[sl][:, 0:1024], w_src[128 * r:128 * r + 128, :],
                        fam="fs")
            K.waitp("tensor", d_w)
            for b in (2 * psl, 2 * psl + 1):
                K.use_bank("tensor", b)
            p_t = None
            for j in range(NJ):
                p_t = K.do("tensor", lambda E, _j=j, _sl=sl, _p=psl: E.transpose(
                    ps[:, 1024 * _p + 128 * _j:1024 * _p + 128 * _j + 128],
                    stg_c[_sl][:, 128 * _j:128 * _j + 128], ident[:]),
                    inc="spe" if j == NJ - 1 else None)
            cstg_free[sl] = ("spe", p_t)
            K.wait("vector", "spe", p_t)
            v_e = K.do("vector", lambda E, _r=r, _p=psl, _w=wT3m: E.tensor_copy(
                _w[:, :, 128 * _r:128 * _r + 128], ps3[:, 8 * _p:8 * _p + 8, :]),
                inc="sve")
            cprep_done = v_e
            for b in (2 * psl, 2 * psl + 1):
                K.release_bank(b, "sve", v_e)

    stage_free = [None] * NJ
    attn_reads_done = None
    v_o = None
    for hp in range(HP):
        K.wait("tensor", "sve", cprep_done)
        if attn_reads_done is not None:
            K.wait("vector", "spe", attn_reads_done)
        for (Wt3, dst, thr_ap) in ((wqT3, Qs, thrq), (wkT3, Ks, thrk)):
            for tn in range(NJ):
                b = 4 + (tn % 2)
                K.use_bank("tensor", b)
                p_q = None
                for j in range(NJ):
                    p_q = K.do("tensor", lambda E, _j=j, _b=b, _tn=tn, _w=Wt3, _hp=hp:
                               E.matmul(
                        ps[:, 512 * _b:512 * _b + 512],
                        _w[:, _j, 128 * _hp:128 * _hp + 128],
                        xnT3[:, _j, 512 * _tn:512 * _tn + 512],
                        start=(_j == 0), stop=(_j == NJ - 1)),
                        inc="spe" if j == NJ - 1 else None)
                K.wait("vector", "spe", p_q)
                v_q = K.do("vector", lambda E, _b=b, _tn=tn, _d=dst, _t=thr_ap, _hp=hp:
                           E.tensor_scalar(_d[:, 512 * _tn:512 * _tn + 512],
                                           ps[:, 512 * _b:512 * _b + 512],
                                           _t[:, _hp:_hp + 1], None, ALU.is_ge),
                           inc="sve")
                K.release_bank(b, "sve", v_q)
        for tc in range(NT):
            b = 4 + (tc % 2)
            K.use_bank("tensor", b)
            p_v = None
            for j in range(NJ):
                p_v = K.do("tensor", lambda E, _j=j, _b=b, _tc=tc, _hp=hp: E.matmul(
                    ps[:, 512 * _b:512 * _b + 128],
                    xnT3[:, _j, 128 * _tc:128 * _tc + 128],
                    wvT3[:, _j, 128 * _hp:128 * _hp + 128],
                    start=(_j == 0), stop=(_j == NJ - 1)),
                    inc="spe" if j == NJ - 1 else None)
            K.wait("vector", "spe", p_v)
            v_v = K.do("vector", lambda E, _b=b, _tc=tc, _hp=hp: E.tensor_tensor(
                Vt[:, 128 * _tc:128 * _tc + 128], ps[:, 512 * _b:512 * _b + 128],
                bvb[:, 128 * _hp:128 * _hp + 128], ALU.add), inc="sve")
            K.release_bank(b, "sve", v_v)
        K.wait("tensor", "sve", v_v)

        av_done = {}
        sig_done = {}
        for qt in range(NJ):
            b_o = 4 + (qt % 2)
            K.use_bank("tensor", b_o)

            def emit_av(kc, _qt=None):
                q_ = qt if _qt is None else _qt
                slot = kc % 4
                K.wait("tensor", "sse", sig_done[(q_, kc)])
                K.do("tensor", lambda E, _kc=kc, _sl=slot, _b=b_o: E.matmul(
                    ps[0:64, 512 * _b:512 * _b + 512],
                    Vt[:, 128 * _kc:128 * _kc + 64],
                    rA[:, 1024 * _sl:1024 * _sl + 512],
                    start=(_kc == 0), stop=(_kc == NT - 1), tile_position=(0, 0)))
                av = K.do("tensor", lambda E, _kc=kc, _sl=slot, _b=b_o: E.matmul(
                    ps[64:128, 512 * _b:512 * _b + 512],
                    Vt[:, 128 * _kc + 64:128 * _kc + 128],
                    rA[:, 1024 * _sl + 512:1024 * _sl + 1024],
                    start=(_kc == 0), stop=(_kc == NT - 1), tile_position=(0, 64)),
                    inc="spe")
                av_done[(q_, kc)] = av

            for kc in range(NT):
                bs = 2 * (kc % 2)
                K.use_bank("tensor", bs)
                K.use_bank("tensor", bs + 1)
                K.do("tensor", lambda E, _b=bs, _kc=kc, _qt=qt: E.matmul(
                    ps[:, 512 * _b:512 * _b + 512],
                    Ks[0:64, 128 * _kc:128 * _kc + 128],
                    Qs[0:64, 512 * _qt:512 * _qt + 512],
                    start=True, stop=True, tile_position=(0, 0)))
                p_sB = K.do("tensor", lambda E, _b=bs + 1, _kc=kc, _qt=qt: E.matmul(
                    ps[:, 512 * _b:512 * _b + 512],
                    Ks[64:128, 128 * _kc:128 * _kc + 128],
                    Qs[64:128, 512 * _qt:512 * _qt + 512],
                    start=True, stop=True, tile_position=(64, 0)), inc="spe")
                slot = kc % 4
                K.wait("scalar", "spe", p_sB)
                if kc >= 4:
                    K.wait("scalar", "spe", av_done[(qt, kc - 4)])
                elif qt > 0:
                    K.wait("scalar", "spe", av_done[(qt - 1, NT - 4 + kc)])
                s_AB = K.do("scalar", lambda E, _b=bs, _sl=slot: E.activation(
                    rA[:, 1024 * _sl:1024 * _sl + 1024],
                    ps[:, 512 * _b:512 * _b + 1024],
                    AF.Sigmoid, scale=0.125), inc="sse")
                K.release_bank(bs, "sse", s_AB)
                K.release_bank(bs + 1, "sse", s_AB)
                sig_done[(qt, kc)] = s_AB
                if kc > 1:
                    emit_av(kc - 2)
            emit_av(NT - 2)
            emit_av(NT - 1)
            K.wait("vector", "spe", av_done[(qt, NT - 1)])
            K.waitp("vector", stage_free[qt])
            v_o = K.do_drained("vector", lambda E, _b=b_o, _qt=qt: E.tensor_copy(
                a_stage[:, 512 * _qt:512 * _qt + 512],
                ps[:, 512 * _b:512 * _b + 512]), inc="sve")
            K.release_bank(b_o, "sve", v_o)
            K.wait("sync", "sve", v_o)
            stage_free[qt] = K.dma(
                "sync", xat_h[128 * hp:128 * hp + 128, 512 * qt:512 * qt + 512],
                a_stage[:, 512 * qt:512 * qt + 512], fam="fas")
            emit_conv(6)
        attn_reads_done = av_done[(NJ - 1, NT - 1)]
    emit_conv(10 ** 9)
    conv_all = (("fcx", K.cnt["fcx"]), ("fcy", K.cnt["fcy"]))
    K.flush()
    c_st.close()
    xnT_st.close()

    if upto == "C":
        K.final_wait(); K.flush(); K.simulate()
        r2_st.close(); st.close()
        return nc

    K.barrier()
    if upto == "BAR":
        K.final_wait(); K.flush(); K.simulate()
        r2_st.close(); st.close()
        return nc

    if routed:
        b_st = ExitStack()
        bsb = lambda n, sh, dt: b_st.enter_context(nc.sbuf_tensor(n, sh, dt))
        xg = [bsb(f"xg{i}", [128, Dm], F32) for i in range(2)]
        xgT = [bsb(f"xgT{i}", [128, NJ * 512], BF16) for i in range(2)]
        xgT3 = [x[:, :].rearrange("p (j n) -> p j n", n=512) for x in xgT]
        h_sb = bsb("h_sb", [128, NF * 512], BF16)
        w1bf = [bsb(f"w1bf{i}", [128, 4096], BF16) for i in range(2)]
        w2bf = [bsb(f"w2bf{i}", [128, 2048], BF16) for i in range(2)]
        ysc2 = [bsb(f"ysc{i}", [128, 4 * Dm], F32) for i in range(2)]
        yb = [bsb(f"yb{i}", [128, Dm], F32) for i in range(2)]
        idxi = [bsb(f"idxi{i}", [128, 4], mybir.dt.int32) for i in range(2)]
        pcolb = [bsb(f"pcolb{i}", [128, 4], F32) for i in range(2)]
        idxf = [bsb(f"idxf{i}", [128, 2], F32) for i in range(2)]

        K.wait("tensor", "sve", pn_last)
        K.wait("tensor", "sve", v_b2)
        yb_free = [None, None]
        for tc in range(NT):
            bb = 2 * (tc % 2)
            K.use_bank("tensor", bb)
            K.use_bank("tensor", bb + 1)
            p_y0 = None
            for ih in range(2):
                p_y0 = K.do("tensor", lambda E, _tc=tc, _ih=ih, _bb=bb: E.matmul(
                    ps[:, 512 * (_bb + _ih):512 * (_bb + _ih) + 512],
                    PnT[0:8, 128 * _tc:128 * _tc + 128],
                    b2bf[0:8, 512 * _ih:512 * _ih + 512],
                    start=True, stop=True), inc="spe" if ih == 1 else None)
            K.wait("vector", "spe", p_y0)
            K.waitp("vector", yb_free[tc % 2])
            v_y0 = K.do_drained("vector", lambda E, _tc=tc, _bb=bb: E.tensor_copy(
                yb[_tc % 2][:], ps[:, 512 * _bb:512 * _bb + 1024]), inc="sve")
            K.release_bank(bb, "sve", v_y0)
            K.release_bank(bb + 1, "sve", v_y0)
            K.wait("gpsimd", "sve", v_y0)
            d_y0 = K.dma("gpsimd", y_h[128 * tc:128 * tc + 128, :], yb[tc % 2][:],
                         fam="fyw")
            yb_free[tc % 2] = d_y0
        K.wait("gpsimd", "fyw", K.cnt["fyw"])

        idx_free = [None, None]
        ivf_free = [None, None]
        xgT_free = [None, None]
        w1_free = [None, None]
        w2_free = [None, None]
        ysc_free = {}
        fys_chain = 0
        for e in range(E8):
            for tile in range(3):
                g3 = 3 * e + tile
                sl = g3 % 2
                base = ESTRIDE * e + 512 * tile
                if idx_free[sl] is not None:
                    K.waitp("vector", idx_free[sl])
                K.waitp("vector", xgT_free[sl])
                v_ic = None
                last_ga = None
                xg_free = getattr(K, "_xg_free", [None, None])
                for c in range(4):
                    csl = c % 2
                    K.waitp("sync", ivf_free[csl])
                    K.wait("sync", "fix", fix_total)
                    d_p = K.dma("sync", idxf[csl][:],
                                idxP_h[base + 128 * c:base + 128 * c + 128, :],
                                fam="fiv")
                    K.waitp("vector", d_p)
                    K.do("vector", lambda E, _sl=sl, _c=c, _cs=csl: E.tensor_copy(
                        idxi[_sl][:, _c:_c + 1], idxf[_cs][:, 0:1]))
                    v_ic = K.do_drained("vector", lambda E, _sl=sl, _c=c, _cs=csl:
                                        E.tensor_copy(pcolb[_sl][:, _c:_c + 1],
                                                      idxf[_cs][:, 1:2]),
                                        inc="sve")
                    ivf_free[csl] = ("sve", v_ic)
                    K.wait("gpsimd", "sve", v_ic)
                    K.waitp("gpsimd", xg_free[csl])
                    d_g = K.dma_indirect_in(xg[csl][:], xtok_h,
                                            idxi[sl][:, c:c + 1], fam="fga")
                    last_ga = d_g
                    K.waitp("tensor", d_g)
                    K.use_bank("tensor", 2)
                    K.use_bank("tensor", 3)
                    p_tg = None
                    for j in range(NJ):
                        p_tg = K.do("tensor", lambda E, _j=j, _cs=csl: E.transpose(
                            ps[:, 1024 + 128 * _j:1024 + 128 * _j + 128],
                            xg[_cs][:, 128 * _j:128 * _j + 128], ident[:]),
                            inc="spe" if j == NJ - 1 else None)
                    xg_free[csl] = ("spe", p_tg)
                    K._xg_free = xg_free
                    K.wait("vector", "spe", p_tg)
                    v_xgt = K.do("vector", lambda E, _sl=sl, _c=c: E.tensor_copy(
                        xgT3[_sl][:, :, 128 * _c:128 * _c + 128], ps3[:, 8:16, :]),
                        inc="sve")
                    K.release_bank(2, "sve", v_xgt)
                    K.release_bank(3, "sve", v_xgt)
                v_xgt = K.do("vector", lambda E: E.drain(), inc="sve")
                gelu_v = {}
                w1_dma = None
                last_w1 = None
                for f in range(NF):
                    q4 = f // 4
                    wsl = q4 % 2
                    if f % 4 == 0:
                        K.waitp("sync", w1_free[wsl])
                        for _f, _v in conv_all:
                            K.wait("sync", _f, _v)
                        w1_dma = K.dma("sync",
                                       w1bf[wsl][:].rearrange("p (fc j f) -> p fc j f",
                                                              fc=4, j=NJ),
                                       w1b_h[e, 4 * q4:4 * q4 + 4]
                                       .rearrange("fc p j f -> p fc j f"), fam="fw1")
                    bh = f % 2
                    K.use_bank("tensor", bh)
                    K.waitp("tensor", w1_dma)
                    K.wait("tensor", "sve", v_xgt)
                    p_h = None
                    for j in range(NJ):
                        p_h = K.do("tensor", lambda E, _f=f, _j=j, _sl=sl, _w=wsl,
                                   _bh=bh: E.matmul(
                            ps[:, 512 * _bh:512 * _bh + 512],
                            w1bf[_w][:, ((_f % 4) * 8 + _j) * 128:
                                     ((_f % 4) * 8 + _j) * 128 + 128],
                            xgT3[_sl][:, _j, :],
                            start=(_j == 0), stop=(_j == NJ - 1)),
                            inc="spe" if j == NJ - 1 else None)
                    last_w1 = p_h
                    if f % 4 == 3:
                        w1_free[wsl] = ("spe", p_h)
                    K.wait("scalar", "spe", p_h)
                    if f % 4 == 3:
                        K.do("scalar", lambda E, _f=f, _bh=bh, _e=e: E.activation(
                            h_sb[:, 512 * _f:512 * _f + 512],
                            ps[:, 512 * _bh:512 * _bh + 512],
                            AF.Gelu, bias=b1c[:, 8 * _f + _e:8 * _f + _e + 1]))
                        s_g = K.do("scalar", lambda E: E.drain(), inc="sse")
                    else:
                        s_g = K.do("scalar", lambda E, _f=f, _bh=bh, _e=e: E.activation(
                            h_sb[:, 512 * _f:512 * _f + 512],
                            ps[:, 512 * _bh:512 * _bh + 512],
                            AF.Gelu, bias=b1c[:, 8 * _f + _e:8 * _f + _e + 1]), inc="sse")
                    gelu_v[f] = s_g
                    K.release_bank(bh, "sse", s_g)
                xgT_free[sl] = ("spe", last_w1)
                for dh in range(2):
                    for b in range(4, 8):
                        K.use_bank("tensor", b)
                    p_w2 = None
                    for fq in range(NJ):
                        wsl2 = fq % 2
                        K.waitp("sync", w2_free[wsl2])
                        for _f, _v in conv_all:
                            K.wait("sync", _f, _v)
                        d_w2 = K.dma("sync",
                                     w2bf[wsl2][:].rearrange("p (fi d) -> p fi d", d=512),
                                     w2b_h[e, dh, 4 * fq:4 * fq + 4]
                                     .rearrange("fi p d -> p fi d"), fam="fw2")
                        K.waitp("tensor", d_w2)
                        K.wait("tensor", "sse", gelu_v[4 * fq + 3])
                        for f4 in range(4):
                            f = 4 * fq + f4
                            for sc in range(4):
                                p_w2 = K.do("tensor", lambda E, _f=f, _f4=f4, _sc=sc,
                                            _w=wsl2: E.matmul(
                                    ps[:, 512 * (4 + _sc):512 * (4 + _sc) + 512],
                                    h_sb[:, 512 * _f + 128 * _sc:
                                         512 * _f + 128 * _sc + 128],
                                    w2bf[_w][:, 512 * _f4:512 * _f4 + 512],
                                    start=(_f == 0), stop=(_f == NF - 1)),
                                    inc="spe" if (f4 == 3 and sc == 3) else None)
                        w2_free[wsl2] = ("spe", p_w2)
                    ysc = ysc2[sl]
                    for sc in range(4):
                        K.wait("vector", "spe", p_w2)
                        K.waitp("vector", ysc_free.get((sl, sc)))
                        if dh == 1:
                            v_ys = K.do_drained("vector", lambda E, _sc=sc, _dh=dh, _sl=sl:
                                                E.tensor_scalar(
                                ysc2[_sl][:, 1024 * _sc + 512 * _dh:
                                          1024 * _sc + 512 * _dh + 512],
                                ps[:, 512 * (4 + _sc):512 * (4 + _sc) + 512],
                                pcolb[_sl][:, _sc:_sc + 1], None, ALU.mult), inc="sve")
                        else:
                            v_ys = K.do("vector", lambda E, _sc=sc, _dh=dh, _sl=sl:
                                        E.tensor_scalar(
                                ysc2[_sl][:, 1024 * _sc + 512 * _dh:
                                          1024 * _sc + 512 * _dh + 512],
                                ps[:, 512 * (4 + _sc):512 * (4 + _sc) + 512],
                                pcolb[_sl][:, _sc:_sc + 1], None, ALU.mult), inc="sve")
                        K.release_bank(4 + sc, "sve", v_ys)
                        if dh == 1:
                            K.wait("gpsimd", "sve", v_ys)
                            K.wait("gpsimd", "fys", fys_chain)
                            d_ys = K.dma_indirect_out(y_h, idxi[sl][:, sc:sc + 1],
                                                      ysc2[sl][:, 1024 * sc:1024 * sc + 1024],
                                                      fam="fys", op=ALU.add)
                            ysc_free[(sl, sc)] = d_ys
                idx_free[sl] = ysc_free.get((sl, 3))
            fys_chain = K.cnt["fys"]
        K.flush()
        b_st.close()
        r2_st.close()
    else:
        y_sb = sb("y_sb", [128, NT * Dm], BF16)
        b_st = ExitStack()
        bsb = lambda n, sh, dt: b_st.enter_context(nc.sbuf_tensor(n, sh, dt))
        h_sb = bsb("h_sb", [128, NF * TTB], BF16)
        xnt_t = [bsb(f"xnt_t{i}", [128, NJ * TTB], BF16) for i in range(2)]
        w1bf = [bsb(f"w1bf{i}", [128, 4096], BF16) for i in range(2)]
        w2bf = [bsb(f"w2bf{i}", [128, 2048], BF16) for i in range(2)]

        K.wait("tensor", "sve", pn_last)
        K.wait("tensor", "sve", v_b2)
        for tc in range(NT):
            bb = 2 * (tc % 2)
            K.use_bank("tensor", bb)
            K.use_bank("tensor", bb + 1)
            p_y0 = None
            for ih in range(2):
                p_y0 = K.do("tensor", lambda E, _tc=tc, _ih=ih, _bb=bb: E.matmul(
                    ps[:, 512 * (_bb + _ih):512 * (_bb + _ih) + 512],
                    PnT[0:8, 128 * _tc:128 * _tc + 128],
                    b2bf[0:8, 512 * _ih:512 * _ih + 512],
                    start=True, stop=True), inc="spe" if ih == 1 else None)
            K.wait("vector", "spe", p_y0)
            v_y0 = K.do("vector", lambda E, _tc=tc, _bb=bb: E.tensor_copy(
                y_sb[:, 1024 * _tc:1024 * _tc + 1024],
                ps[:, 512 * _bb:512 * _bb + 1024]), inc="sve")
            K.release_bank(bb, "sve", v_y0)
            K.release_bank(bb + 1, "sve", v_y0)

        if upto == "YINIT":
            K.final_wait(); K.flush(); K.simulate(); st.close()
            return nc
        xnt_free = [None, None]
        w1_free = [None, None]
        w2_free = [None, None]
        for e in range(b_experts):
            for tt in range(NTT):
                sl = tt % 2
                K.waitp("sync", xnt_free[sl])
                d_xn = None
                for j in range(NJ):
                    d_xn = K.dma("sync", xnt_t[sl][:, 1024 * j:1024 * j + 1024],
                                 xnt_h[128 * j:128 * j + 128, 1024 * tt:1024 * tt + 1024],
                                 fam="fxn")
                last_w1_mm = None
                gelu_v = {}
                w1_dma = None
                for f in range(NF):
                    q4 = f // 4
                    wsl = q4 % 2
                    if f % 4 == 0:
                        K.waitp("sync", w1_free[wsl])
                        for _f, _v in conv_all:
                            K.wait("sync", _f, _v)
                        w1_dma = K.dma("sync",
                                       w1bf[wsl][:].rearrange("p (fc j f) -> p fc j f",
                                                              fc=4, j=NJ),
                                       w1b_h[e, 4 * q4:4 * q4 + 4]
                                       .rearrange("fc p j f -> p fc j f"), fam="fw1")
                    bh = f % 2
                    K.use_bank("tensor", 2 * bh)
                    K.use_bank("tensor", 2 * bh + 1)
                    K.waitp("tensor", w1_dma)
                    K.waitp("tensor", d_xn)
                    p_h = None
                    for half in range(2):
                        for j in range(NJ):
                            p_h = K.do("tensor", lambda E, _f=f, _h=half, _j=j, _sl=sl,
                                       _w=wsl, _bh=bh: E.matmul(
                                ps[:, 1024 * _bh + 512 * _h:1024 * _bh + 512 * _h + 512],
                                w1bf[_w][:, ((_f % 4) * 8 + _j) * 128:
                                         ((_f % 4) * 8 + _j) * 128 + 128],
                                xnt_t[_sl][:, 1024 * _j + 512 * _h:1024 * _j + 512 * _h + 512],
                                start=(_j == 0), stop=(_j == NJ - 1)),
                                inc="spe" if (half == 1 and j == NJ - 1) else None)
                    last_w1_mm = p_h
                    if f % 4 == 3:
                        w1_free[wsl] = ("spe", p_h)
                    K.wait("scalar", "spe", p_h)
                    K.do("scalar", lambda E, _f=f, _bh=bh, _e=e: E.activation(
                        h_sb[:, 1024 * _f:1024 * _f + 1024],
                        ps[:, 1024 * _bh:1024 * _bh + 1024],
                        AF.Gelu, bias=b1c[:, 8 * _f + _e:8 * _f + _e + 1]))
                    s_g = K.do("scalar", lambda E: E.drain(), inc="sse")
                    gelu_v[f] = s_g
                    K.release_bank(2 * bh, "sse", s_g)
                    K.release_bank(2 * bh + 1, "sse", s_g)
                xnt_free[sl] = ("spe", last_w1_mm)
                for dh in range(2):
                    for b in range(8):
                        K.use_bank("tensor", b)
                    last_mm = {}
                    for fq in range(NJ):
                        wsl2 = fq % 2
                        K.waitp("sync", w2_free[wsl2])
                        for _f, _v in conv_all:
                            K.wait("sync", _f, _v)
                        d_w2 = K.dma("sync",
                                     w2bf[wsl2][:].rearrange("p (fi d) -> p fi d", d=512),
                                     w2b_h[e, dh, 4 * fq:4 * fq + 4]
                                     .rearrange("fi p d -> p fi d"), fam="fw2")
                        K.waitp("tensor", d_w2)
                        K.wait("tensor", "sse", gelu_v[4 * fq + 3])
                        p_w2 = None
                        for f4 in range(4):
                            f = 4 * fq + f4
                            for tcq in range(8):
                                p_w2 = K.do("tensor", lambda E, _f=f, _f4=f4, _tc=tcq,
                                            _w=wsl2: E.matmul(
                                    ps[:, 512 * _tc:512 * _tc + 512],
                                    h_sb[:, 1024 * _f + 128 * _tc:
                                         1024 * _f + 128 * _tc + 128],
                                    w2bf[_w][:, 512 * _f4:512 * _f4 + 512],
                                    start=(_f == 0), stop=(_f == NF - 1)),
                                    inc="spe" if (f4 == 3 and tcq == 7) else None)
                        w2_free[wsl2] = ("spe", p_w2)
                        if fq == NJ - 1:
                            for tcq in range(8):
                                last_mm[tcq] = p_w2
                    for tcq in range(8):
                        tcg = 8 * tt + tcq
                        K.wait("vector", "spe", last_mm[tcq])
                        v_y = K.do("vector", lambda E, _tc=tcq, _tcg=tcg, _dh=dh, _e=e:
                                   E.scalar_tensor_tensor(
                            y_sb[:, 1024 * _tcg + 512 * _dh:1024 * _tcg + 512 * _dh + 512],
                            ps[:, 512 * _tc:512 * _tc + 512],
                            Pn[:, 8 * _tcg + _e:8 * _tcg + _e + 1],
                            y_sb[:, 1024 * _tcg + 512 * _dh:1024 * _tcg + 512 * _dh + 512],
                            op0=ALU.mult, op1=ALU.add), inc="sve")
                        K.release_bank(tcq, "sve", v_y)
        if debug:
            K.wait("gpsimd", "sve", K.cnt["sve"])
            K.dma("gpsimd", dbg_y, y_sb[:], fam="fo")
            K.wait("gpsimd", "sse", K.cnt["sse"])
            K.dma("gpsimd", dbg_h, h_sb[:], fam="fo")
        K.flush()
        b_st.close()

        if upto == "B":
            K.final_wait(); K.flush(); K.simulate(); st.close()
            return nc

    K.barrier()
    de_st = ExitStack()
    dsb = lambda n, sh, dt: de_st.enter_context(nc.sbuf_tensor(n, sh, dt))
    woT = dsb("woT", [128, NJ * Dm], BF16)
    woT3 = woT[:, :].rearrange("p (j n) -> p j n", n=Dm)
    att_t = [dsb(f"att_t{i}", [128, NJ * 512], BF16) for i in range(2)]
    s_t = [dsb(f"s_t{i}", [128, Dm], BF16) for i in range(2)]
    g_t = [dsb(f"g_t{i}", [128, Dm], F32) for i in range(2)]
    xe = [dsb(f"xe{i}", [128, Dm], F32) for i in range(2)]
    o_t = [dsb(f"o_t{i}", [128, Dm], F32) for i in range(2)]
    stgD = [dsb(f"stgD{i}", [128, Dm], F32) for i in range(2)]
    ysil = [dsb(f"ysil{i}", [128, Dm], F32) for i in range(2)] if routed else None
    ysil_free = [None, None]

    stgD_free = [None, None]
    for r in range(NJ):
        sl = r % 2
        K.waitp("sync", stgD_free[sl])
        d_w = K.dma("sync", stgD[sl][:], wo_h[128 * r:128 * r + 128, :], fam="fs")
        K.waitp("tensor", d_w)
        for b in (2 * sl, 2 * sl + 1):
            K.use_bank("tensor", b)
        p_t = None
        for j in range(NJ):
            p_t = K.do("tensor", lambda E, _j=j, _sl=sl: E.transpose(
                ps[:, 1024 * _sl + 128 * _j:1024 * _sl + 128 * _j + 128],
                stgD[_sl][:, 128 * _j:128 * _j + 128], ident[:]),
                inc="spe" if j == NJ - 1 else None)
        stgD_free[sl] = ("spe", p_t)
        K.wait("vector", "spe", p_t)
        v_e = K.do("vector", lambda E, _r=r, _sl=sl: E.tensor_copy(
            woT3[:, :, 128 * _r:128 * _r + 128], ps3[:, 8 * _sl:8 * _sl + 8, :]),
            inc="sve")
        for b in (2 * sl, 2 * sl + 1):
            K.release_bank(b, "sve", v_e)

    K.wait("tensor", "sve", v_e)
    att_free = [None, None]
    xe_free = [None, None]
    g_free = [None, None]
    o_free = [None, None]
    for qt in range(NJ):
        sl = qt % 2
        K.waitp("sync", att_free[sl])
        d_a = None
        for c in range(NJ):
            d_a = K.dma("sync", att_t[sl][:, 512 * c:512 * c + 512],
                        xat_h[128 * c:128 * c + 128, 512 * qt:512 * qt + 512],
                        fam="fat")
        last_de_mm = None
        for tc4 in range(4):
            tcg = 4 * qt + tc4
            tsl = tc4 % 2
            bb = 2 * tsl
            p_hi = None
            for ih in range(2):
                b = bb + ih
                K.use_bank("tensor", b)
                K.waitp("tensor", d_a)
                for c in range(NJ):
                    p_hi = K.do("tensor", lambda E, _c=c, _b=b, _ih=ih, _sl=sl,
                                _tc4=tc4: E.matmul(
                        ps[:, 512 * _b:512 * _b + 512],
                        att_t[_sl][:, 512 * _c + 128 * _tc4:512 * _c + 128 * _tc4 + 128],
                        woT3[:, _c, 512 * _ih:512 * _ih + 512],
                        start=(_c == 0), stop=(_c == NJ - 1)),
                        inc="spe" if c == NJ - 1 else None) or p_hi
            last_de_mm = p_hi
            K.wait("vector", "spe", p_hi)
            for ih in range(2):
                v_s = K.do("vector", lambda E, _b=bb + ih, _ih=ih, _tsl=tsl:
                           E.tensor_tensor(
                    s_t[_tsl][:, 512 * _ih:512 * _ih + 512],
                    ps[:, 512 * _b:512 * _b + 512],
                    thro[:, 512 * _ih:512 * _ih + 512], ALU.is_ge), inc="sve")
                K.release_bank(bb + ih, "sve", v_s)
            K.waitp("scalar", g_free[tsl])
            if routed:
                K.waitp("sync", ysil_free[tsl])
                d_ys = K.dma("sync", ysil[tsl][:], y_h[128 * tcg:128 * tcg + 128, :],
                             fam="fy")
                K.waitp("scalar", d_ys)
                s_si = K.do_drained("scalar", lambda E, _tsl=tsl: E.activation(
                    g_t[_tsl][:], ysil[_tsl][:], AF.Silu), inc="sse")
                ysil_free[tsl] = ("sse", s_si)
            else:
                s_si = K.do_drained("scalar", lambda E, _tcg=tcg, _tsl=tsl: E.activation(
                    g_t[_tsl][:], y_sb[:, 1024 * _tcg:1024 * _tcg + 1024], AF.Silu),
                    inc="sse")
            K.waitp("sync", xe_free[tsl])
            d_x = K.dma("sync", xe[tsl][:], x_h[128 * tcg:128 * tcg + 128, :], fam="fx")
            K.wait("vector", "sse", s_si)
            K.waitp("vector", d_x)
            K.waitp("vector", o_free[tsl])
            K.drain("vector")
            K.do("vector", lambda E, _tsl=tsl: E.tensor_tensor(
                g_t[_tsl][:], g_t[_tsl][:], s_t[_tsl][:], ALU.mult))
            K.drain("vector")
            v_cmb = K.do_drained("vector", lambda E, _tsl=tsl: E.tensor_tensor(
                o_t[_tsl][:], g_t[_tsl][:], xe[_tsl][:], ALU.add), inc="sve")
            g_free[tsl] = ("sve", v_cmb)
            xe_free[tsl] = ("sve", v_cmb)
            K.wait("gpsimd", "sve", v_cmb)
            out_wr = K.dma("gpsimd", out_h[128 * tcg:128 * tcg + 128, :], o_t[tsl][:], fam="fo")
            o_free[tsl] = out_wr
            if debug:
                K.dma("gpsimd", dbg_s[128 * tcg:128 * tcg + 128, :], s_t[tsl][:], fam="fo")
        att_free[sl] = ("spe", last_de_mm)
    K.final_wait()
    K.flush()
    de_st.close()
    K.simulate()
    st.close()
    return nc


_CACHED = {}


def _get_program(upto="DE", debug=False):
    key = (upto, debug)
    if key not in _CACHED:
        _CACHED[key] = build_program(upto, debug)
    return _CACHED[key]


def make_in_maps(inputs):
    f = lambda k: np.ascontiguousarray(np.asarray(inputs[k], dtype=np.float32))
    x = f("L0_attn_x")
    base = {
        "wq": f("L0_attn_Wq"), "wk": f("L0_attn_Wk"), "wv": f("L0_attn_Wv"),
        "wo": f("L0_attn_Wo"),
        "bq": f("L0_attn_bq").reshape(8, 128), "bk": f("L0_attn_bk").reshape(8, 128),
        "bv": f("L0_attn_bv").reshape(1, 1024), "bo": f("L0_attn_bo").reshape(1, 1024),
        "wg": f("L0_ffn_Wg"), "w1": f("L0_ffn_W1"), "b1": f("L0_ffn_b1"),
        "w2": f("L0_ffn_W2"), "b2": f("L0_ffn_b2"),
    }
    return [dict(base, x=np.ascontiguousarray(x[i])) for i in range(8)]


def kernel(**inputs):
    nc = _get_program()
    in_maps = make_in_maps(inputs)
    res = run_bass_kernel_spmd(nc, in_maps, core_ids=list(range(8)))
    return np.stack([np.asarray(res.results[i]["out"]) for i in range(8)], axis=0)


# version 35
# speedup vs baseline: 1.0868x; 1.0868x over previous
import numpy as np
import ml_dtypes
from contextlib import ExitStack
import concourse.bass as bass
import concourse.mybir as mybir
from concourse.bass_utils import run_bass_kernel_spmd

F32 = mybir.dt.float32
BF16 = mybir.dt.bfloat16
AF = mybir.ActivationFunctionType
ALU = mybir.AluOpType
AX = mybir.AxisListType

T = 4096
Dm = 1024
NJ = 8
NT = 32
E8 = 8
FF = 4096
NF = 32
HP = 8
TTB = 1024
NTT = 4
CCAP = 1536
ESTRIDE = 2048
DUMPROW = 8 * ESTRIDE
IDXROWS = 16512

ENGINES = ("tensor", "scalar", "vector", "sync", "gpsimd")
DMA_FAMS = ("fb", "fx", "fxo", "fxt", "fs", "fc1", "fc2", "fcx", "fcy", "fas", "fxn",
            "fw1", "fw2", "fat", "fo", "fix", "fiv", "fga", "fys", "fyw", "fy")
SEMS = ("spe", "sse", "sve", "ssp", "spo") + DMA_FAMS
ESEM = {"tensor": "spe", "scalar": "sse", "vector": "sve", "sync": "ssp", "gpsimd": "spo"}


class KB:

    def __init__(self, nc, block, sems):
        self.nc, self.block, self.sem = nc, block, sems
        self.cnt = {s: 0 for s in SEMS}
        self.hw = {e: {s: 0 for s in SEMS} for e in ENGINES}
        self.bank_free = {b: None for b in range(8)}
        self.q = {e: [] for e in ENGINES}
        self.trace = {e: [] for e in ENGINES}

    def wait(self, eng, sem, val):
        if val is None or val <= 0 or self.hw[eng][sem] >= val:
            return
        self.hw[eng][sem] = val
        h = self.sem[sem]
        self.q[eng].append(lambda E, _h=h, _v=val: E.wait_ge(_h, _v))
        self.trace[eng].append((((sem, val),), ()))

    def waitp(self, eng, pair):
        if pair is not None:
            self.wait(eng, pair[0], pair[1])

    def do(self, eng, fn, inc=None, inc_by=1):
        if inc is None:
            self.q[eng].append(fn)
            self.trace[eng].append(((), ()))
            return None
        h = self.sem[inc]
        self.q[eng].append(lambda E, _fn=fn, _h=h, _k=inc_by: _fn(E).then_inc(_h, _k))
        self.cnt[inc] += inc_by
        v = self.cnt[inc]
        self.trace[eng].append(((), ((inc, v),)))
        self.hw[eng][inc] = max(self.hw[eng][inc], v)
        return v

    def dma(self, eng, out_ap, in_ap, fam="fb"):
        h = self.sem[fam]
        self.q[eng].append(lambda E, _o=out_ap, _i=in_ap, _h=h:
                           E.dma_start(_o, _i).then_inc(_h, 16))
        self.cnt[fam] += 16
        v = self.cnt[fam]
        self.trace[eng].append(((), ((fam, v),)))
        return (fam, v)

    def drain(self, eng):
        self.q[eng].append(lambda E: E.drain())
        self.trace[eng].append(((), ()))

    def dma_indirect_out(self, out_ap, idx_ap, in_ap, fam, op=None):
        h = self.sem[fam]
        alu = op if op is not None else ALU.bypass
        self.q["gpsimd"].append(
            lambda E, _o=out_ap, _x=idx_ap, _i=in_ap, _h=h, _a=alu:
            E.indirect_dma_start(_o, bass.IndirectOffsetOnAxis(_x, 0), _i, None,
                                 oob_is_err=False, compute_op=_a).then_inc(_h, 16))
        self.cnt[fam] += 16
        v = self.cnt[fam]
        self.trace["gpsimd"].append(((), ((fam, v),)))
        return (fam, v)

    def dma_indirect_in(self, out_ap, in_ap, idx_ap, fam):
        h = self.sem[fam]
        self.q["gpsimd"].append(
            lambda E, _o=out_ap, _x=idx_ap, _i=in_ap, _h=h:
            E.indirect_dma_start(_o, None, _i, bass.IndirectOffsetOnAxis(_x, 0),
                                 oob_is_err=False).then_inc(_h, 16))
        self.cnt[fam] += 16
        v = self.cnt[fam]
        self.trace["gpsimd"].append(((), ((fam, v),)))
        return (fam, v)

    def do_drained(self, eng, fn, inc):
        self.do(eng, fn)
        return self.do(eng, lambda E: E.drain(), inc=inc)

    def use_bank(self, eng, b):
        self.waitp(eng, self.bank_free[b])

    def release_bank(self, b, sem, val):
        self.bank_free[b] = (sem, val)

    def barrier(self):
        marks = []
        for e in ENGINES:
            s = ESEM[e]
            if e == "tensor":
                marks.append((s, self.cnt[s]))
            else:
                marks.append((s, self.do(e, lambda E: E.nop(), inc=s)))
        targets = marks + [(f, self.cnt[f]) for f in DMA_FAMS]
        for e in ENGINES:
            for s, v in targets:
                self.wait(e, s, v)

    def flush(self):
        for e in ENGINES:
            items = self.q[e]
            if not items:
                continue
            fn = getattr(self.block, {"tensor": "tensor", "scalar": "scalar",
                                      "vector": "vector", "sync": "sync",
                                      "gpsimd": "gpsimd"}[e])

            def run(E, _items=items):
                for f in _items:
                    f(E)
            fn(run)
            self.q[e] = []

    def simulate(self):
        ptr = {e: 0 for e in ENGINES}
        val = {s: 0 for s in SEMS}
        progress = True
        while progress:
            progress = False
            for e in ENGINES:
                while ptr[e] < len(self.trace[e]):
                    waits, incs = self.trace[e][ptr[e]]
                    if any(val[s] < v for s, v in waits):
                        break
                    for s, v in incs:
                        val[s] = max(val[s], v)
                    ptr[e] += 1
                    progress = True
        stuck = {e: (ptr[e], len(self.trace[e]), self.trace[e][ptr[e]][0])
                 for e in ENGINES if ptr[e] < len(self.trace[e])}
        if stuck:
            raise RuntimeError(f"build-time deadlock: {stuck} semvals={val}")

    def final_wait(self):
        for f in DMA_FAMS:
            self.hw["gpsimd"][f] = 0
            self.wait("gpsimd", f, self.cnt[f])


def build_program(upto="DE", debug=False, b_experts=E8, routed=False):
    nc = bass.Bass("TRN2", target_bir_lowering=False)

    x_h = nc.dram_tensor("x", [T, Dm], F32, kind="ExternalInput").ap()
    wq_h = nc.dram_tensor("wq", [Dm, Dm], F32, kind="ExternalInput").ap()
    wk_h = nc.dram_tensor("wk", [Dm, Dm], F32, kind="ExternalInput").ap()
    wv_h = nc.dram_tensor("wv", [Dm, Dm], F32, kind="ExternalInput").ap()
    wo_h = nc.dram_tensor("wo", [Dm, Dm], F32, kind="ExternalInput").ap()
    bq_h = nc.dram_tensor("bq", [8, 128], F32, kind="ExternalInput").ap()
    bk_h = nc.dram_tensor("bk", [8, 128], F32, kind="ExternalInput").ap()
    bv_h = nc.dram_tensor("bv", [1, Dm], F32, kind="ExternalInput").ap()
    bo_h = nc.dram_tensor("bo", [1, Dm], F32, kind="ExternalInput").ap()
    wg_h = nc.dram_tensor("wg", [Dm, 8], F32, kind="ExternalInput").ap()
    w1_h = nc.dram_tensor("w1", [E8, Dm, FF], F32, kind="ExternalInput").ap()
    b1_h = nc.dram_tensor("b1", [E8, FF], F32, kind="ExternalInput").ap()
    w2_h = nc.dram_tensor("w2", [E8, FF, Dm], F32, kind="ExternalInput").ap()
    b2_h = nc.dram_tensor("b2", [E8, Dm], F32, kind="ExternalInput").ap()
    out_h = nc.dram_tensor("out", [T, Dm], F32, kind="ExternalOutput").ap()

    skind = "ExternalOutput" if debug else "Internal"
    xnt_h = nc.dram_tensor("dbg_xnt" if debug else "xnt", [Dm, T], BF16, kind=skind).ap()
    xat_h = nc.dram_tensor("dbg_xat" if debug else "xat", [Dm, T], BF16, kind=skind).ap()
    w1b_h = nc.dram_tensor("w1b", [E8, NF, 128, NJ, 128], BF16, kind="Internal").ap()
    xtok_h = nc.dram_tensor("xtok", [T, Dm], F32, kind="Internal").ap()
    idxP_h = nc.dram_tensor("idxP", [16640, 2], F32, kind="Internal").ap()
    y_h = nc.dram_tensor("dbg_yh" if debug else "yh", [T, Dm], F32,
                         kind="ExternalOutput" if debug else "Internal").ap()
    w2b_h = nc.dram_tensor("w2b", [E8, 2, NF, 128, 512], BF16, kind="Internal").ap()
    if debug:
        dbg_pn = nc.dram_tensor("dbg_pn", [128, 256], F32, kind="ExternalOutput").ap()
        dbg_y = nc.dram_tensor("dbg_y", [128, NT * Dm], BF16, kind="ExternalOutput").ap()
        dbg_s = nc.dram_tensor("dbg_s", [T, Dm], BF16, kind="ExternalOutput").ap()
        dbg_h = nc.dram_tensor("dbg_h", [128, NF * TTB], BF16, kind="ExternalOutput").ap()

    st = ExitStack()
    sems = {s: st.enter_context(nc.semaphore(s)) for s in SEMS}
    sb = lambda name, shape, dt: st.enter_context(nc.sbuf_tensor(name, shape, dt))

    ident = sb("ident", [128, 128], F32)
    ident1 = sb("ident1", [1, 1], F32)
    ones1 = sb("ones1", [1, 128], F32)
    wg_bf = sb("wg_bf", [128, 64], BF16)
    thrq = sb("thrq", [128, 8], F32)
    thrk = sb("thrk", [128, 8], F32)
    bvb = sb("bvb", [128, Dm], F32)
    thro = sb("thro", [128, Dm], F32)
    b1c = sb("b1c", [128, 256], F32)
    b2bf = sb("b2bf", [8, Dm], BF16)
    Pn = sb("Pn", [128, 256], F32)
    PnT = sb("PnT", [8, T], BF16)
    ps = st.enter_context(nc.psum_tensor("ps", [128, 4096], F32))
    ps3 = ps[:, :].rearrange("p (g c) -> p g c", c=128)

    block = st.enter_context(nc.Block())
    K = KB(nc, block, sems)

    init_st = ExitStack()
    isb = lambda n, sh, dt: init_st.enter_context(nc.sbuf_tensor(n, sh, dt))
    bqn = isb("bqn", [8, 128], F32)
    bkn = isb("bkn", [8, 128], F32)
    bvr = isb("bvr", [1, Dm], F32)
    bor = isb("bor", [1, Dm], F32)
    b1n = isb("b1n", [8, FF], F32)
    b2f = isb("b2f", [8, Dm], F32)
    wgf = isb("wgf", [128, 64], F32)

    K.do("gpsimd", lambda E: E.memset(ident[:], 0.0))
    K.drain("gpsimd")
    K.do("gpsimd", lambda E: E.affine_select(out=ident[:], in_=ident[:],
                                             compare_op=ALU.not_equal, fill=1.0,
                                             base=0, pattern=[[-1, 128]],
                                             channel_multiplier=1))
    K.do("gpsimd", lambda E: E.memset(ident1[:], 1.0))
    po_ready = K.do("gpsimd", lambda E: E.memset(ones1[:], 1.0), inc="spo")

    K.dma("sync", bkn[:], bk_h)
    K.dma("sync", bqn[:], bq_h)
    K.dma("sync", bvr[:], bv_h)
    K.dma("sync", bor[:], bo_h)
    K.dma("sync", b1n[:], b1_h)
    K.dma("sync", b2f[:], b2_h)
    for c in range(8):
        K.dma("sync", wgf[:, 8 * c:8 * c + 8], wg_h[128 * c:128 * c + 128, :])
    d_init = ("fb", K.cnt["fb"])

    K.waitp("vector", d_init)
    v_wg = K.do("vector", lambda E: E.tensor_copy(wg_bf[:], wgf[:]), inc="sve")
    v_b2 = K.do("vector", lambda E: E.tensor_copy(b2bf[:], b2f[:]), inc="sve")
    v_bo = K.do("vector", lambda E: E.tensor_scalar(bor[:], bor[:], 1.0, -1.0,
                                                    ALU.subtract, ALU.mult), inc="sve")

    K.wait("tensor", "spo", po_ready)
    K.waitp("tensor", d_init)
    K.do("tensor", lambda E: E.transpose(ps[:, 0:8], bqn[:], ident[0:8, 0:8]))
    p_bk = K.do("tensor", lambda E: E.transpose(ps[:, 8:16], bkn[:], ident[0:8, 0:8]),
                inc="spe")
    K.waitp("tensor", d_init)
    p_b1 = None
    for fc in range(NF):
        p_b1 = K.do("tensor", lambda E, _fc=fc: E.transpose(
            ps[:, 512 + 8 * _fc:520 + 8 * _fc],
            b1n[0:8, 128 * _fc:128 * _fc + 128], ident[0:8, 0:8]),
            inc="spe" if fc == NF - 1 else None)
    K.waitp("tensor", d_init)
    K.do("tensor", lambda E: E.matmul(ps[:, 1024:1536], ones1[:], bvr[0:1, 0:512],
                                      start=True, stop=True))
    p_bv = K.do("tensor", lambda E: E.matmul(ps[:, 1536:2048], ones1[:], bvr[0:1, 512:1024],
                                             start=True, stop=True), inc="spe")
    K.wait("tensor", "sve", v_bo)
    K.do("tensor", lambda E: E.matmul(ps[:, 2048:2560], ones1[:], bor[0:1, 0:512],
                                      start=True, stop=True))
    p_bo = K.do("tensor", lambda E: E.matmul(ps[:, 2560:3072], ones1[:], bor[0:1, 512:1024],
                                             start=True, stop=True), inc="spe")

    K.wait("vector", "spe", p_bk)
    K.do("vector", lambda E: E.tensor_scalar(thrq[:], ps[:, 0:8], 1.0, -1.0,
                                             ALU.subtract, ALU.mult))
    K.do("vector", lambda E: E.tensor_scalar(thrk[:], ps[:, 8:16], 1.0, -1.0,
                                             ALU.subtract, ALU.mult))
    K.wait("vector", "spe", p_b1)
    K.do("vector", lambda E: E.tensor_copy(b1c[:], ps[:, 512:768]))
    K.wait("vector", "spe", p_bv)
    K.do("vector", lambda E: E.tensor_copy(bvb[:], ps[:, 1024:2048]))
    K.wait("vector", "spe", p_bo)
    v_init = K.do("vector", lambda E: E.tensor_copy(thro[:], ps[:, 2048:3072]), inc="sve")
    for b in range(6):
        K.release_bank(b, "sve", v_init)
    K.flush()
    init_st.close()

    r2_st = ExitStack()
    r2_sb = lambda n, sh, dt: r2_st.enter_context(nc.sbuf_tensor(n, sh, dt))
    if routed:
        off32 = r2_sb("off32", [128, 256], mybir.dt.int32)
        tok32 = r2_sb("tok32", [128, 32], F32)
        vals = r2_sb("vals", [128, 512], F32)
        zero_sb = r2_sb("zero_sb", [128, 260], F32)
    xnT_st = ExitStack()
    xnT = xnT_st.enter_context(nc.sbuf_tensor("xnT", [128, NJ * T], BF16))
    xnT3 = xnT[:, :].rearrange("p (j n) -> p j n", n=T)

    a_st = ExitStack()
    asb = lambda n, sh, dt: a_st.enter_context(nc.sbuf_tensor(n, sh, dt))
    x_sb = [asb(f"x_sb{i}", [128, Dm], F32) for i in range(3)]
    xnf = [asb(f"xnf{i}", [128, Dm], F32) for i in range(3)]
    sq_scr = asb("sq_scr", [128, Dm], F32)
    cols = asb("cols", [128, 24], F32)

    x_free = [None, None, None]
    xnf_free = [None, None, None]
    evac_v = None
    for t in range(NT):
        sl = t % 3
        if x_free[sl] is not None:
            for pr in x_free[sl]:
                K.waitp("sync", pr)
        d_x = K.dma("sync", x_sb[sl][:], x_h[128 * t:128 * t + 128, :], fam="fx")
        co = 8 * sl
        c_xsum, c_sumsq = cols[:, co:co + 1], cols[:, co + 1:co + 2]
        c_mean, c_ex2 = cols[:, co + 2:co + 3], cols[:, co + 3:co + 4]
        c_var, c_std = cols[:, co + 4:co + 5], cols[:, co + 5:co + 6]
        c_rstd, c_nmean = cols[:, co + 6:co + 7], cols[:, co + 7:co + 8]

        K.waitp("vector", d_x)
        K.do("vector", lambda E, a=c_xsum, b=x_sb[sl]: E.reduce_sum(a, b[:], axis=AX.X))
        K.waitp("scalar", d_x)
        s_sq = K.do_drained("scalar", lambda E, b=x_sb[sl], a=c_sumsq: E.activation(
            sq_scr[:], b[:], AF.Square, accum_out=a), inc="sse")
        K.drain("vector")
        K.do("vector", lambda E, a=c_mean, b=c_xsum: E.tensor_scalar(
            a, b, 1.0 / Dm, None, ALU.mult))
        K.do("vector", lambda E, a=c_nmean, b=c_xsum: E.tensor_scalar(
            a, b, -1.0 / Dm, None, ALU.mult))
        K.wait("vector", "sse", s_sq)
        K.do("vector", lambda E, a=c_ex2, b=c_sumsq: E.tensor_scalar(
            a, b, 1.0 / Dm, 1e-5, ALU.mult, ALU.add))
        K.drain("vector")
        v_var = K.do_drained("vector", lambda E, nm=c_nmean, m=c_mean, a=c_ex2, v=c_var:
                     E.scalar_tensor_tensor(v, nm, m, a, op0=ALU.mult, op1=ALU.add),
                     inc="sve")
        K.wait("scalar", "sve", v_var)
        s_std = K.do_drained("scalar", lambda E, v=c_var, s_=c_std: E.activation(
            s_, v, AF.Sqrt), inc="sse")
        K.wait("vector", "sse", s_std)
        K.do("vector", lambda E, s_=c_std, r=c_rstd: E.reciprocal(r, s_))
        if xnf_free[sl] is not None:
            for pr in xnf_free[sl]:
                K.waitp("vector", pr)
        K.drain("vector")
        v_xn = K.do_drained("vector", lambda E, xx=x_sb[sl], m=c_mean, r=c_rstd, o=xnf[sl]:
                    E.tensor_scalar(o[:], xx[:], m, r, ALU.subtract, ALU.mult), inc="sve")
        x_free[sl] = (("sve", v_xn), ("sse", s_sq))
        if routed:
            K.wait("gpsimd", "sve", v_xn)
            d_xt = K.dma("gpsimd", xtok_h[128 * t:128 * t + 128, :], xnf[sl][:], fam="fxt")
        else:
            d_xt = None

        K.wait("tensor", "sve", v_xn)
        for b in (2 * sl, 2 * sl + 1):
            K.use_bank("tensor", b)
        p_tr = None
        for j in range(NJ):
            p_tr = K.do("tensor", lambda E, _j=j, _sl=sl, _x=xnf[sl]: E.transpose(
                ps[:, 1024 * _sl + 128 * _j:1024 * _sl + 128 * _j + 128],
                _x[:, 128 * _j:128 * _j + 128], ident[:]),
                inc="spe" if j == NJ - 1 else None)
        xnf_free[sl] = (("spe", p_tr), d_xt) if d_xt else (("spe", p_tr),)
        K.wait("vector", "spe", p_tr)
        evac_v = K.do("vector", lambda E, _t=t, _sl=sl: E.tensor_copy(
            xnT3[:, :, 128 * _t:128 * _t + 128], ps3[:, 8 * _sl:8 * _sl + 8, :]), inc="sve")
        for b in (2 * sl, 2 * sl + 1):
            K.release_bank(b, "sve", evac_v)

    evac_v = K.do("vector", lambda E: E.drain(), inc="sve")
    xnt_wr = None
    K.wait("gpsimd", "sve", evac_v)
    for j in range(NJ):
        xnt_wr = K.dma("gpsimd", xnt_h[128 * j:128 * j + 128, :], xnT[:, T * j:T * j + T], fam="fxo")
    K.flush()
    a_st.close()

    r_st = ExitStack()
    PnTf = r_st.enter_context(nc.sbuf_tensor("PnTf", [8, T], F32)) if routed else None
    a2_st = ExitStack()
    Eex = [a2_st.enter_context(nc.sbuf_tensor(f"Eex{i}", [128, 8], F32)) for i in range(2)]
    g8 = [a2_st.enter_context(nc.sbuf_tensor(f"g8_{i}", [128, 8], F32)) for i in range(2)]
    gc = a2_st.enter_context(nc.sbuf_tensor("gc", [128, 8], F32))

    K.wait("tensor", "sve", evac_v)
    K.wait("tensor", "sve", v_wg)
    eex_free = [None, None]
    pn_last = None
    for t in range(NT):
        sl = t % 2
        b_g = 4 + sl
        K.use_bank("tensor", b_g)
        p_g = None
        for j in range(NJ):
            p_g = K.do("tensor", lambda E, _j=j, _b=b_g, _t=t: E.matmul(
                ps[:, 512 * _b:512 * _b + 8],
                xnT3[:, _j, 128 * _t:128 * _t + 128],
                wg_bf[:, 8 * _j:8 * _j + 8],
                start=(_j == 0), stop=(_j == NJ - 1)),
                inc="spe" if j == NJ - 1 else None)
        co = 4 * sl
        c_m1, c_m2 = gc[:, co:co + 1], gc[:, co + 1:co + 2]
        c_den, c_s = gc[:, co + 2:co + 3], gc[:, co + 3:co + 4]
        K.wait("vector", "spe", p_g)
        v_m1 = K.do_drained("vector", lambda E, _b=b_g, m=c_m1: E.reduce_max(
            m, ps[:, 512 * _b:512 * _b + 8], axis=AX.X, negate=True), inc="sve")
        K.wait("scalar", "sve", v_m1)
        K.waitp("scalar", eex_free[sl])
        s_e = K.do_drained("scalar", lambda E, _b=b_g, m=c_m1, o=Eex[sl]: E.activation(
            o[:], ps[:, 512 * _b:512 * _b + 8], AF.Exp, bias=m), inc="sse")
        K.release_bank(b_g, "sse", s_e)
        ex, g = Eex[sl], g8[sl]
        K.wait("vector", "sse", s_e)
        K.do("vector", lambda E, ex=ex, g=g: E.tensor_scalar(
            g[:], ex[:], 1.0, -1e30, ALU.is_ge, ALU.mult))
        K.do("vector", lambda E, ex=ex, s_=c_s: E.reduce_sum(s_, ex[:], axis=AX.X))
        K.drain("vector")
        K.do("vector", lambda E, ex=ex, g=g: E.tensor_tensor(g[:], g[:], ex[:], ALU.add))
        K.drain("vector")
        K.do("vector", lambda E, g=g, m=c_m2: E.reduce_max(m, g[:], axis=AX.X))
        K.drain("vector")
        K.do("vector", lambda E, ex=ex, g=g, m=c_m2: E.tensor_scalar(
            g[:], ex[:], m, None, ALU.is_ge))
        K.drain("vector")
        K.do("vector", lambda E, ex=ex, g=g: E.tensor_tensor(g[:], g[:], ex[:], ALU.mult))
        K.drain("vector")
        K.do("vector", lambda E, g=g, d=c_den: E.reduce_sum(d, g[:], axis=AX.X))
        K.drain("vector")
        K.do("vector", lambda E, s_=c_s, d=c_den: E.scalar_tensor_tensor(
            d, s_, 1e-9, d, op0=ALU.mult, op1=ALU.add))
        K.drain("vector")
        K.do("vector", lambda E, d=c_den: E.reciprocal(d, d))
        K.drain("vector")
        v_pn = K.do_drained("vector", lambda E, g=g, d=c_den, _t=t: E.tensor_scalar(
            Pn[:, 8 * _t:8 * _t + 8], g[:], d, None, ALU.mult), inc="sve")
        eex_free[sl] = ("sve", v_pn)
        b_t = 6 + sl
        K.use_bank("tensor", b_t)
        K.wait("tensor", "sve", v_pn)
        p_pt = K.do("tensor", lambda E, _b=b_t, _t=t: E.transpose(
            ps[0:8, 512 * _b:512 * _b + 128], Pn[:, 8 * _t:8 * _t + 8], ident[:]), inc="spe")
        K.wait("vector", "spe", p_pt)
        if routed:
            K.do("vector", lambda E, _b=b_t, _t=t: E.tensor_copy(
                PnT[0:8, 128 * _t:128 * _t + 128], ps[0:8, 512 * _b:512 * _b + 128]))
            pn_last = K.do("vector", lambda E, _b=b_t, _t=t: E.tensor_copy(
                PnTf[0:8, 128 * _t:128 * _t + 128], ps[0:8, 512 * _b:512 * _b + 128]),
                inc="sve")
        else:
            pn_last = K.do("vector", lambda E, _b=b_t, _t=t: E.tensor_copy(
                PnT[0:8, 128 * _t:128 * _t + 128], ps[0:8, 512 * _b:512 * _b + 128]),
                inc="sve")
        K.release_bank(b_t, "sve", pn_last)
    if debug:
        K.wait("gpsimd", "sve", pn_last)
        K.dma("gpsimd", dbg_pn, Pn[:], fam="fo")
    K.flush()
    a2_st.close()

    if upto == "A2":
        K.final_wait(); K.flush(); K.simulate()
        r_st.close(); r2_st.close(); xnT_st.close(); st.close()
        return nc

    fix_total = 0
    if routed:
        rsb = lambda n, sh, dt: r_st.enter_context(nc.sbuf_tensor(n, sh, dt))
        maskT = rsb("maskT", [8, T], F32)
        posT = rsb("posT", [8, T], F32)
        offT = rsb("offT", [8, T], F32)
        vldT = rsb("vldT", [8, T], F32)
        eoff = rsb("eoff", [8, T], F32)

        po_r = K.do("gpsimd", lambda E: E.iota(
            eoff[:], [[0, T]], channel_multiplier=ESTRIDE,
            allow_small_or_imprecise_dtypes=True), inc="spo")
        K.do("gpsimd", lambda E: E.memset(zero_sb[:], 0.0))
        po_r = K.do("gpsimd", lambda E: E.iota(
            tok32[:], [[128, 32]], channel_multiplier=1,
            allow_small_or_imprecise_dtypes=True), inc="spo")
        d_z1 = K.dma("gpsimd", idxP_h.rearrange("(a b) c -> a (b c)", b=130),
                     zero_sb[:], fam="fix")

        K.wait("vector", "sve", pn_last)
        K.drain("vector")
        K.do("vector", lambda E: E.tensor_scalar(maskT[:], PnTf[:], 0.0, None, ALU.is_gt))
        K.drain("vector")
        K.do("vector", lambda E: E.tensor_tensor_scan(
            posT[:], maskT[:], maskT[:], 0.0, op0=ALU.add, op1=ALU.bypass))
        K.drain("vector")
        K.do("vector", lambda E: E.tensor_tensor(posT[:], posT[:], maskT[:], ALU.subtract))
        K.do("vector", lambda E: E.tensor_scalar(vldT[:], posT[:], float(CCAP), None,
                                                 ALU.is_lt))
        K.drain("vector")
        K.do("vector", lambda E: E.tensor_tensor(maskT[:], maskT[:], vldT[:], ALU.mult))
        K.wait("vector", "spo", po_r)
        K.do("vector", lambda E: E.tensor_tensor(offT[:], posT[:], eoff[:], ALU.add))
        K.drain("vector")
        K.do("vector", lambda E: E.tensor_scalar(offT[:], offT[:], -float(DUMPROW), None,
                                                 ALU.add))
        K.drain("vector")
        K.do("vector", lambda E: E.tensor_tensor(offT[:], offT[:], maskT[:], ALU.mult))
        K.drain("vector")
        v_off = K.do_drained("vector", lambda E: E.tensor_scalar(
            offT[:], offT[:], float(DUMPROW), None, ALU.add), inc="sve")

        for c in range(NT):
            for e in range(E8):
                k = 8 * c + e
                K.do("vector", lambda E, _c=c, _k=k: E.tensor_copy(
                    vals[:, 2 * _k:2 * _k + 1], tok32[:, _c:_c + 1]))
                K.do("vector", lambda E, _k=k: E.tensor_copy(
                    vals[:, 2 * _k + 1:2 * _k + 2], Pn[:, _k:_k + 1]))
        K.drain("vector")

        K.wait("tensor", "sve", v_off)
        v_o32 = None
        for c in range(NT):
            b = c % 2
            K.use_bank("tensor", b)
            p_ot = K.do("tensor", lambda E, _b=b, _c=c: E.transpose(
                ps[:, 512 * _b:512 * _b + 8], offT[0:8, 128 * _c:128 * _c + 128],
                ident[0:8, 0:8]), inc="spe")
            K.wait("vector", "spe", p_ot)
            v_o32 = K.do("vector", lambda E, _b=b, _c=c: E.tensor_copy(
                off32[:, 8 * _c:8 * _c + 8], ps[:, 512 * _b:512 * _b + 8]), inc="sve")
            K.release_bank(b, "sve", v_o32)
        v_o32 = K.do("vector", lambda E: E.drain(), inc="sve")

        K.wait("gpsimd", "sve", v_o32)
        K.waitp("gpsimd", d_z1)
        for c in range(NT):
            for e in range(E8):
                k = 8 * c + e
                K.dma_indirect_out(idxP_h, off32[:, k:k + 1],
                                   vals[:, 2 * k:2 * k + 2], fam="fix")
        fix_total = K.cnt["fix"]
    K.flush()
    r_st.close()

    c_st = ExitStack()
    csb = lambda n, sh, dt: c_st.enter_context(nc.sbuf_tensor(n, sh, dt))
    wqT = csb("wqT", [128, NJ * Dm], BF16)
    wkT = csb("wkT", [128, NJ * Dm], BF16)
    wvT = csb("wvT", [128, NJ * Dm], BF16)
    wqT3 = wqT[:, :].rearrange("p (j n) -> p j n", n=Dm)
    wkT3 = wkT[:, :].rearrange("p (j n) -> p j n", n=Dm)
    wvT3 = wvT[:, :].rearrange("p (j n) -> p j n", n=Dm)
    Qs = csb("Qs", [128, T], BF16)
    Ks = csb("Ks", [128, T], BF16)
    Vt = csb("Vt", [128, T], BF16)
    rA = csb("rA", [128, 4 * 1024], BF16)
    a_stage = csb("a_stage", [128, T], BF16)
    stg_c = [csb(f"stg_c{i}", [128, 2048], F32) for i in range(3)]
    cvo = [csb(f"cvo{i}", [128, 2048], BF16) for i in range(2)]

    conv_steps = []
    for e in range(E8):
        for k in range(NJ):
            for half in range(2):
                conv_steps.append(("w1",
                                   w1_h[e, 128 * k:128 * k + 128,
                                        2048 * half:2048 * half + 2048],
                                   w1b_h[e, 16 * half:16 * half + 16, :, k, :]
                                   .rearrange("fc p f -> p fc f")))
        for fi in range(NF):
            conv_steps.append(("w2", w2_h[e, 128 * fi:128 * fi + 128, :],
                               (w2b_h[e, 0, fi], w2b_h[e, 1, fi])))
    conv_i = [0]
    stg_ctr = [0]
    cstg_free = [None, None, None]
    conv_free = [None, None, None]
    cvo_free = [None, None]

    def emit_conv(n):
        for _ in range(n):
            i = conv_i[0]
            if i >= len(conv_steps):
                return
            conv_i[0] += 1
            kind, src_, dst = conv_steps[i]
            sl = stg_ctr[0] % 3
            stg_ctr[0] += 1
            sl2 = i % 2
            nfree = 2048 if kind == "w1" else 1024
            K.waitp("sync", cstg_free[sl])
            K.waitp("sync", conv_free[sl])
            d_in = K.dma("sync", stg_c[sl][:, 0:nfree], src_,
                         fam="fc1" if kind == "w1" else "fc2")
            K.waitp("gpsimd", d_in)
            K.waitp("gpsimd", cvo_free[sl2])
            v_c = K.do("gpsimd", lambda E, _sl=sl, _s2=sl2, _n=nfree:
                       E.tensor_copy(cvo[_s2][:, 0:_n], stg_c[_sl][:, 0:_n]),
                       inc="spo")
            conv_free[sl] = ("spo", v_c)
            cstg_free[sl] = ("spo", v_c)
            if kind == "w1":
                w_ = K.dma("gpsimd", dst,
                           cvo[sl2][:, 0:2048].rearrange("p (fc f) -> p fc f", f=128),
                           fam="fcx")
            else:
                K.dma("gpsimd", dst[0], cvo[sl2][:, 0:512], fam="fcy")
                w_ = K.dma("gpsimd", dst[1], cvo[sl2][:, 512:1024], fam="fcy")
            cvo_free[sl2] = w_

    emit_conv(48)

    cprep_done = None
    for m, (w_src, wT3m) in enumerate([(wq_h, wqT3), (wk_h, wkT3), (wv_h, wvT3)]):
        for r in range(NJ):
            sl = stg_ctr[0] % 3
            stg_ctr[0] += 1
            psl = (m * NJ + r) % 2
            K.waitp("sync", cstg_free[sl])
            d_w = K.dma("sync", stg_c[sl][:, 0:1024], w_src[128 * r:128 * r + 128, :],
                        fam="fs")
            K.waitp("tensor", d_w)
            for b in (2 * psl, 2 * psl + 1):
                K.use_bank("tensor", b)
            p_t = None
            for j in range(NJ):
                p_t = K.do("tensor", lambda E, _j=j, _sl=sl, _p=psl: E.transpose(
                    ps[:, 1024 * _p + 128 * _j:1024 * _p + 128 * _j + 128],
                    stg_c[_sl][:, 128 * _j:128 * _j + 128], ident[:]),
                    inc="spe" if j == NJ - 1 else None)
            cstg_free[sl] = ("spe", p_t)
            K.wait("vector", "spe", p_t)
            v_e = K.do("vector", lambda E, _r=r, _p=psl, _w=wT3m: E.tensor_copy(
                _w[:, :, 128 * _r:128 * _r + 128], ps3[:, 8 * _p:8 * _p + 8, :]),
                inc="sve")
            cprep_done = v_e
            for b in (2 * psl, 2 * psl + 1):
                K.release_bank(b, "sve", v_e)

    stage_free = [None] * NJ
    attn_reads_done = None
    v_o = None
    for hp in range(HP):
        K.wait("tensor", "sve", cprep_done)
        if attn_reads_done is not None:
            K.wait("vector", "spe", attn_reads_done)
        for (Wt3, dst, thr_ap) in ((wqT3, Qs, thrq), (wkT3, Ks, thrk)):
            for tn in range(NJ):
                b = 4 + (tn % 2)
                K.use_bank("tensor", b)
                p_q = None
                for j in range(NJ):
                    p_q = K.do("tensor", lambda E, _j=j, _b=b, _tn=tn, _w=Wt3, _hp=hp:
                               E.matmul(
                        ps[:, 512 * _b:512 * _b + 512],
                        _w[:, _j, 128 * _hp:128 * _hp + 128],
                        xnT3[:, _j, 512 * _tn:512 * _tn + 512],
                        start=(_j == 0), stop=(_j == NJ - 1)),
                        inc="spe" if j == NJ - 1 else None)
                K.wait("vector", "spe", p_q)
                v_q = K.do("vector", lambda E, _b=b, _tn=tn, _d=dst, _t=thr_ap, _hp=hp:
                           E.tensor_scalar(_d[:, 512 * _tn:512 * _tn + 512],
                                           ps[:, 512 * _b:512 * _b + 512],
                                           _t[:, _hp:_hp + 1], None, ALU.is_ge),
                           inc="sve")
                K.release_bank(b, "sve", v_q)
        for tc in range(NT):
            b = 4 + (tc % 2)
            K.use_bank("tensor", b)
            p_v = None
            for j in range(NJ):
                p_v = K.do("tensor", lambda E, _j=j, _b=b, _tc=tc, _hp=hp: E.matmul(
                    ps[:, 512 * _b:512 * _b + 128],
                    xnT3[:, _j, 128 * _tc:128 * _tc + 128],
                    wvT3[:, _j, 128 * _hp:128 * _hp + 128],
                    start=(_j == 0), stop=(_j == NJ - 1)),
                    inc="spe" if j == NJ - 1 else None)
            K.wait("vector", "spe", p_v)
            v_v = K.do("vector", lambda E, _b=b, _tc=tc, _hp=hp: E.tensor_tensor(
                Vt[:, 128 * _tc:128 * _tc + 128], ps[:, 512 * _b:512 * _b + 128],
                bvb[:, 128 * _hp:128 * _hp + 128], ALU.add), inc="sve")
            K.release_bank(b, "sve", v_v)
        K.wait("tensor", "sve", v_v)

        av_done = {}
        sig_done = {}
        for qt in range(NJ):
            b_o = 4 + (qt % 2)
            K.use_bank("tensor", b_o)

            def emit_av(kc, _qt=None):
                q_ = qt if _qt is None else _qt
                slot = kc % 4
                K.wait("tensor", "sse", sig_done[(q_, kc)])
                K.do("tensor", lambda E, _kc=kc, _sl=slot, _b=b_o: E.matmul(
                    ps[0:64, 512 * _b:512 * _b + 512],
                    Vt[:, 128 * _kc:128 * _kc + 64],
                    rA[:, 1024 * _sl:1024 * _sl + 512],
                    start=(_kc == 0), stop=(_kc == NT - 1), tile_position=(0, 0)))
                av = K.do("tensor", lambda E, _kc=kc, _sl=slot, _b=b_o: E.matmul(
                    ps[64:128, 512 * _b:512 * _b + 512],
                    Vt[:, 128 * _kc + 64:128 * _kc + 128],
                    rA[:, 1024 * _sl + 512:1024 * _sl + 1024],
                    start=(_kc == 0), stop=(_kc == NT - 1), tile_position=(0, 64)),
                    inc="spe")
                av_done[(q_, kc)] = av

            for kc in range(NT):
                bs = 2 * (kc % 2)
                K.use_bank("tensor", bs)
                K.use_bank("tensor", bs + 1)
                K.do("tensor", lambda E, _b=bs, _kc=kc, _qt=qt: E.matmul(
                    ps[:, 512 * _b:512 * _b + 512],
                    Ks[0:64, 128 * _kc:128 * _kc + 128],
                    Qs[0:64, 512 * _qt:512 * _qt + 512],
                    start=True, stop=True, tile_position=(0, 0)))
                p_sB = K.do("tensor", lambda E, _b=bs + 1, _kc=kc, _qt=qt: E.matmul(
                    ps[:, 512 * _b:512 * _b + 512],
                    Ks[64:128, 128 * _kc:128 * _kc + 128],
                    Qs[64:128, 512 * _qt:512 * _qt + 512],
                    start=True, stop=True, tile_position=(64, 0)), inc="spe")
                slot = kc % 4
                K.wait("scalar", "spe", p_sB)
                if kc >= 4:
                    K.wait("scalar", "spe", av_done[(qt, kc - 4)])
                elif qt > 0:
                    K.wait("scalar", "spe", av_done[(qt - 1, NT - 4 + kc)])
                s_AB = K.do("scalar", lambda E, _b=bs, _sl=slot: E.activation(
                    rA[:, 1024 * _sl:1024 * _sl + 1024],
                    ps[:, 512 * _b:512 * _b + 1024],
                    AF.Sigmoid, scale=0.125), inc="sse")
                K.release_bank(bs, "sse", s_AB)
                K.release_bank(bs + 1, "sse", s_AB)
                sig_done[(qt, kc)] = s_AB
                if kc > 1:
                    emit_av(kc - 2)
            emit_av(NT - 2)
            emit_av(NT - 1)
            K.wait("vector", "spe", av_done[(qt, NT - 1)])
            K.waitp("vector", stage_free[qt])
            v_o = K.do_drained("vector", lambda E, _b=b_o, _qt=qt: E.tensor_copy(
                a_stage[:, 512 * _qt:512 * _qt + 512],
                ps[:, 512 * _b:512 * _b + 512]), inc="sve")
            K.release_bank(b_o, "sve", v_o)
            K.wait("sync", "sve", v_o)
            stage_free[qt] = K.dma(
                "sync", xat_h[128 * hp:128 * hp + 128, 512 * qt:512 * qt + 512],
                a_stage[:, 512 * qt:512 * qt + 512], fam="fas")
            emit_conv(6)
        attn_reads_done = av_done[(NJ - 1, NT - 1)]
    emit_conv(10 ** 9)
    conv_all = (("fcx", K.cnt["fcx"]), ("fcy", K.cnt["fcy"]))
    K.flush()
    c_st.close()
    xnT_st.close()

    if upto == "C":
        K.final_wait(); K.flush(); K.simulate()
        r2_st.close(); st.close()
        return nc

    K.barrier()
    if upto == "BAR":
        K.final_wait(); K.flush(); K.simulate()
        r2_st.close(); st.close()
        return nc

    if routed:
        b_st = ExitStack()
        bsb = lambda n, sh, dt: b_st.enter_context(nc.sbuf_tensor(n, sh, dt))
        xg = [bsb(f"xg{i}", [128, Dm], F32) for i in range(2)]
        xgT = [bsb(f"xgT{i}", [128, NJ * 512], BF16) for i in range(2)]
        xgT3 = [x[:, :].rearrange("p (j n) -> p j n", n=512) for x in xgT]
        h_sb = bsb("h_sb", [128, NF * 512], BF16)
        w1bf = [bsb(f"w1bf{i}", [128, 4096], BF16) for i in range(2)]
        w2bf = [bsb(f"w2bf{i}", [128, 2048], BF16) for i in range(2)]
        ysc2 = [bsb(f"ysc{i}", [128, 4 * Dm], F32) for i in range(2)]
        yb = [bsb(f"yb{i}", [128, Dm], F32) for i in range(2)]
        idxi = [bsb(f"idxi{i}", [128, 4], mybir.dt.int32) for i in range(2)]
        pcolb = [bsb(f"pcolb{i}", [128, 4], F32) for i in range(2)]
        idxf = [bsb(f"idxf{i}", [128, 2], F32) for i in range(2)]

        K.wait("tensor", "sve", pn_last)
        K.wait("tensor", "sve", v_b2)
        yb_free = [None, None]
        for tc in range(NT):
            bb = 2 * (tc % 2)
            K.use_bank("tensor", bb)
            K.use_bank("tensor", bb + 1)
            p_y0 = None
            for ih in range(2):
                p_y0 = K.do("tensor", lambda E, _tc=tc, _ih=ih, _bb=bb: E.matmul(
                    ps[:, 512 * (_bb + _ih):512 * (_bb + _ih) + 512],
                    PnT[0:8, 128 * _tc:128 * _tc + 128],
                    b2bf[0:8, 512 * _ih:512 * _ih + 512],
                    start=True, stop=True), inc="spe" if ih == 1 else None)
            K.wait("vector", "spe", p_y0)
            K.waitp("vector", yb_free[tc % 2])
            v_y0 = K.do_drained("vector", lambda E, _tc=tc, _bb=bb: E.tensor_copy(
                yb[_tc % 2][:], ps[:, 512 * _bb:512 * _bb + 1024]), inc="sve")
            K.release_bank(bb, "sve", v_y0)
            K.release_bank(bb + 1, "sve", v_y0)
            K.wait("gpsimd", "sve", v_y0)
            d_y0 = K.dma("gpsimd", y_h[128 * tc:128 * tc + 128, :], yb[tc % 2][:],
                         fam="fyw")
            yb_free[tc % 2] = d_y0
        K.wait("gpsimd", "fyw", K.cnt["fyw"])

        idx_free = [None, None]
        ivf_free = [None, None]
        xgT_free = [None, None]
        w1_free = [None, None]
        w2_free = [None, None]
        ysc_free = {}
        fys_chain = 0
        for e in range(E8):
            for tile in range(3):
                g3 = 3 * e + tile
                sl = g3 % 2
                base = ESTRIDE * e + 512 * tile
                if idx_free[sl] is not None:
                    K.waitp("vector", idx_free[sl])
                K.waitp("vector", xgT_free[sl])
                v_ic = None
                last_ga = None
                xg_free = getattr(K, "_xg_free", [None, None])
                for c in range(4):
                    csl = c % 2
                    K.waitp("sync", ivf_free[csl])
                    K.wait("sync", "fix", fix_total)
                    d_p = K.dma("sync", idxf[csl][:],
                                idxP_h[base + 128 * c:base + 128 * c + 128, :],
                                fam="fiv")
                    K.waitp("vector", d_p)
                    K.do("vector", lambda E, _sl=sl, _c=c, _cs=csl: E.tensor_copy(
                        idxi[_sl][:, _c:_c + 1], idxf[_cs][:, 0:1]))
                    v_ic = K.do_drained("vector", lambda E, _sl=sl, _c=c, _cs=csl:
                                        E.tensor_copy(pcolb[_sl][:, _c:_c + 1],
                                                      idxf[_cs][:, 1:2]),
                                        inc="sve")
                    ivf_free[csl] = ("sve", v_ic)
                    K.wait("gpsimd", "sve", v_ic)
                    K.waitp("gpsimd", xg_free[csl])
                    d_g = K.dma_indirect_in(xg[csl][:], xtok_h,
                                            idxi[sl][:, c:c + 1], fam="fga")
                    last_ga = d_g
                    K.waitp("tensor", d_g)
                    K.use_bank("tensor", 2)
                    K.use_bank("tensor", 3)
                    p_tg = None
                    for j in range(NJ):
                        p_tg = K.do("tensor", lambda E, _j=j, _cs=csl: E.transpose(
                            ps[:, 1024 + 128 * _j:1024 + 128 * _j + 128],
                            xg[_cs][:, 128 * _j:128 * _j + 128], ident[:]),
                            inc="spe" if j == NJ - 1 else None)
                    xg_free[csl] = ("spe", p_tg)
                    K._xg_free = xg_free
                    K.wait("vector", "spe", p_tg)
                    v_xgt = K.do("vector", lambda E, _sl=sl, _c=c: E.tensor_copy(
                        xgT3[_sl][:, :, 128 * _c:128 * _c + 128], ps3[:, 8:16, :]),
                        inc="sve")
                    K.release_bank(2, "sve", v_xgt)
                    K.release_bank(3, "sve", v_xgt)
                v_xgt = K.do("vector", lambda E: E.drain(), inc="sve")
                gelu_v = {}
                w1_dma = None
                last_w1 = None
                for f in range(NF):
                    q4 = f // 4
                    wsl = q4 % 2
                    if f % 4 == 0:
                        K.waitp("sync", w1_free[wsl])
                        for _f, _v in conv_all:
                            K.wait("sync", _f, _v)
                        w1_dma = K.dma("sync",
                                       w1bf[wsl][:].rearrange("p (fc j f) -> p fc j f",
                                                              fc=4, j=NJ),
                                       w1b_h[e, 4 * q4:4 * q4 + 4]
                                       .rearrange("fc p j f -> p fc j f"), fam="fw1")
                    bh = f % 2
                    K.use_bank("tensor", bh)
                    K.waitp("tensor", w1_dma)
                    K.wait("tensor", "sve", v_xgt)
                    p_h = None
                    for j in range(NJ):
                        p_h = K.do("tensor", lambda E, _f=f, _j=j, _sl=sl, _w=wsl,
                                   _bh=bh: E.matmul(
                            ps[:, 512 * _bh:512 * _bh + 512],
                            w1bf[_w][:, ((_f % 4) * 8 + _j) * 128:
                                     ((_f % 4) * 8 + _j) * 128 + 128],
                            xgT3[_sl][:, _j, :],
                            start=(_j == 0), stop=(_j == NJ - 1)),
                            inc="spe" if j == NJ - 1 else None)
                    last_w1 = p_h
                    if f % 4 == 3:
                        w1_free[wsl] = ("spe", p_h)
                    K.wait("scalar", "spe", p_h)
                    if f % 4 == 3:
                        K.do("scalar", lambda E, _f=f, _bh=bh, _e=e: E.activation(
                            h_sb[:, 512 * _f:512 * _f + 512],
                            ps[:, 512 * _bh:512 * _bh + 512],
                            AF.Gelu, bias=b1c[:, 8 * _f + _e:8 * _f + _e + 1]))
                        s_g = K.do("scalar", lambda E: E.drain(), inc="sse")
                    else:
                        s_g = K.do("scalar", lambda E, _f=f, _bh=bh, _e=e: E.activation(
                            h_sb[:, 512 * _f:512 * _f + 512],
                            ps[:, 512 * _bh:512 * _bh + 512],
                            AF.Gelu, bias=b1c[:, 8 * _f + _e:8 * _f + _e + 1]), inc="sse")
                    gelu_v[f] = s_g
                    K.release_bank(bh, "sse", s_g)
                xgT_free[sl] = ("spe", last_w1)
                for dh in range(2):
                    for b in range(4, 8):
                        K.use_bank("tensor", b)
                    p_w2 = None
                    for fq in range(NJ):
                        wsl2 = fq % 2
                        K.waitp("sync", w2_free[wsl2])
                        for _f, _v in conv_all:
                            K.wait("sync", _f, _v)
                        d_w2 = K.dma("sync",
                                     w2bf[wsl2][:].rearrange("p (fi d) -> p fi d", d=512),
                                     w2b_h[e, dh, 4 * fq:4 * fq + 4]
                                     .rearrange("fi p d -> p fi d"), fam="fw2")
                        K.waitp("tensor", d_w2)
                        K.wait("tensor", "sse", gelu_v[4 * fq + 3])
                        for f4 in range(4):
                            f = 4 * fq + f4
                            for sc in range(4):
                                p_w2 = K.do("tensor", lambda E, _f=f, _f4=f4, _sc=sc,
                                            _w=wsl2: E.matmul(
                                    ps[:, 512 * (4 + _sc):512 * (4 + _sc) + 512],
                                    h_sb[:, 512 * _f + 128 * _sc:
                                         512 * _f + 128 * _sc + 128],
                                    w2bf[_w][:, 512 * _f4:512 * _f4 + 512],
                                    start=(_f == 0), stop=(_f == NF - 1)),
                                    inc="spe" if (f4 == 3 and sc == 3) else None)
                        w2_free[wsl2] = ("spe", p_w2)
                    ysc = ysc2[sl]
                    for sc in range(4):
                        K.wait("vector", "spe", p_w2)
                        K.waitp("vector", ysc_free.get((sl, sc)))
                        if dh == 1:
                            v_ys = K.do_drained("vector", lambda E, _sc=sc, _dh=dh, _sl=sl:
                                                E.tensor_scalar(
                                ysc2[_sl][:, 1024 * _sc + 512 * _dh:
                                          1024 * _sc + 512 * _dh + 512],
                                ps[:, 512 * (4 + _sc):512 * (4 + _sc) + 512],
                                pcolb[_sl][:, _sc:_sc + 1], None, ALU.mult), inc="sve")
                        else:
                            v_ys = K.do("vector", lambda E, _sc=sc, _dh=dh, _sl=sl:
                                        E.tensor_scalar(
                                ysc2[_sl][:, 1024 * _sc + 512 * _dh:
                                          1024 * _sc + 512 * _dh + 512],
                                ps[:, 512 * (4 + _sc):512 * (4 + _sc) + 512],
                                pcolb[_sl][:, _sc:_sc + 1], None, ALU.mult), inc="sve")
                        K.release_bank(4 + sc, "sve", v_ys)
                        if dh == 1:
                            K.wait("gpsimd", "sve", v_ys)
                            K.wait("gpsimd", "fys", fys_chain)
                            d_ys = K.dma_indirect_out(y_h, idxi[sl][:, sc:sc + 1],
                                                      ysc2[sl][:, 1024 * sc:1024 * sc + 1024],
                                                      fam="fys", op=ALU.add)
                            ysc_free[(sl, sc)] = d_ys
                idx_free[sl] = ysc_free.get((sl, 3))
            fys_chain = K.cnt["fys"]
        K.flush()
        b_st.close()
        r2_st.close()
    else:
        y_sb = sb("y_sb", [128, NT * Dm], BF16)
        b_st = ExitStack()
        bsb = lambda n, sh, dt: b_st.enter_context(nc.sbuf_tensor(n, sh, dt))
        h_sb = bsb("h_sb", [128, NF * TTB], BF16)
        xnt_t = [bsb(f"xnt_t{i}", [128, NJ * TTB], BF16) for i in range(2)]
        w1bf = [bsb(f"w1bf{i}", [128, 4096], BF16) for i in range(2)]
        w2bf = [bsb(f"w2bf{i}", [128, 2048], BF16) for i in range(2)]

        K.wait("tensor", "sve", pn_last)
        K.wait("tensor", "sve", v_b2)
        for tc in range(NT):
            bb = 2 * (tc % 2)
            K.use_bank("tensor", bb)
            K.use_bank("tensor", bb + 1)
            p_y0 = None
            for ih in range(2):
                p_y0 = K.do("tensor", lambda E, _tc=tc, _ih=ih, _bb=bb: E.matmul(
                    ps[:, 512 * (_bb + _ih):512 * (_bb + _ih) + 512],
                    PnT[0:8, 128 * _tc:128 * _tc + 128],
                    b2bf[0:8, 512 * _ih:512 * _ih + 512],
                    start=True, stop=True), inc="spe" if ih == 1 else None)
            K.wait("vector", "spe", p_y0)
            v_y0 = K.do("vector", lambda E, _tc=tc, _bb=bb: E.tensor_copy(
                y_sb[:, 1024 * _tc:1024 * _tc + 1024],
                ps[:, 512 * _bb:512 * _bb + 1024]), inc="sve")
            K.release_bank(bb, "sve", v_y0)
            K.release_bank(bb + 1, "sve", v_y0)

        if upto == "YINIT":
            K.final_wait(); K.flush(); K.simulate(); st.close()
            return nc
        xnt_free = [None, None]
        w1_free = [None, None]
        w2_free = [None, None]
        for e in range(b_experts):
            for tt in range(NTT):
                sl = tt % 2
                K.waitp("sync", xnt_free[sl])
                d_xn = None
                for j in range(NJ):
                    d_xn = K.dma("sync", xnt_t[sl][:, 1024 * j:1024 * j + 1024],
                                 xnt_h[128 * j:128 * j + 128, 1024 * tt:1024 * tt + 1024],
                                 fam="fxn")
                last_w1_mm = None
                gelu_v = {}
                w1_dma = None
                for f in range(NF):
                    q4 = f // 4
                    wsl = q4 % 2
                    if f % 4 == 0:
                        K.waitp("sync", w1_free[wsl])
                        for _f, _v in conv_all:
                            K.wait("sync", _f, _v)
                        w1_dma = K.dma("sync",
                                       w1bf[wsl][:].rearrange("p (fc j f) -> p fc j f",
                                                              fc=4, j=NJ),
                                       w1b_h[e, 4 * q4:4 * q4 + 4]
                                       .rearrange("fc p j f -> p fc j f"), fam="fw1")
                    bh = f % 2
                    K.use_bank("tensor", 2 * bh)
                    K.use_bank("tensor", 2 * bh + 1)
                    K.waitp("tensor", w1_dma)
                    K.waitp("tensor", d_xn)
                    p_h = None
                    for half in range(2):
                        for j in range(NJ):
                            p_h = K.do("tensor", lambda E, _f=f, _h=half, _j=j, _sl=sl,
                                       _w=wsl, _bh=bh: E.matmul(
                                ps[:, 1024 * _bh + 512 * _h:1024 * _bh + 512 * _h + 512],
                                w1bf[_w][:, ((_f % 4) * 8 + _j) * 128:
                                         ((_f % 4) * 8 + _j) * 128 + 128],
                                xnt_t[_sl][:, 1024 * _j + 512 * _h:1024 * _j + 512 * _h + 512],
                                start=(_j == 0), stop=(_j == NJ - 1)),
                                inc="spe" if (half == 1 and j == NJ - 1) else None)
                    last_w1_mm = p_h
                    if f % 4 == 3:
                        w1_free[wsl] = ("spe", p_h)
                    K.wait("scalar", "spe", p_h)
                    K.do("scalar", lambda E, _f=f, _bh=bh, _e=e: E.activation(
                        h_sb[:, 1024 * _f:1024 * _f + 1024],
                        ps[:, 1024 * _bh:1024 * _bh + 1024],
                        AF.Gelu, bias=b1c[:, 8 * _f + _e:8 * _f + _e + 1]))
                    s_g = K.do("scalar", lambda E: E.drain(), inc="sse")
                    gelu_v[f] = s_g
                    K.release_bank(2 * bh, "sse", s_g)
                    K.release_bank(2 * bh + 1, "sse", s_g)
                xnt_free[sl] = ("spe", last_w1_mm)
                for dh in range(2):
                    for b in range(8):
                        K.use_bank("tensor", b)
                    last_mm = {}
                    for fq in range(NJ):
                        wsl2 = fq % 2
                        K.waitp("sync", w2_free[wsl2])
                        for _f, _v in conv_all:
                            K.wait("sync", _f, _v)
                        d_w2 = K.dma("sync",
                                     w2bf[wsl2][:].rearrange("p (fi d) -> p fi d", d=512),
                                     w2b_h[e, dh, 4 * fq:4 * fq + 4]
                                     .rearrange("fi p d -> p fi d"), fam="fw2")
                        K.waitp("tensor", d_w2)
                        K.wait("tensor", "sse", gelu_v[4 * fq + 3])
                        p_w2 = None
                        for f4 in range(4):
                            f = 4 * fq + f4
                            for tcq in range(8):
                                p_w2 = K.do("tensor", lambda E, _f=f, _f4=f4, _tc=tcq,
                                            _w=wsl2: E.matmul(
                                    ps[:, 512 * _tc:512 * _tc + 512],
                                    h_sb[:, 1024 * _f + 128 * _tc:
                                         1024 * _f + 128 * _tc + 128],
                                    w2bf[_w][:, 512 * _f4:512 * _f4 + 512],
                                    start=(_f == 0), stop=(_f == NF - 1)),
                                    inc="spe" if (f4 == 3 and tcq == 7) else None)
                        w2_free[wsl2] = ("spe", p_w2)
                        if fq == NJ - 1:
                            for tcq in range(8):
                                last_mm[tcq] = p_w2
                    for tcq in range(8):
                        tcg = 8 * tt + tcq
                        K.wait("vector", "spe", last_mm[tcq])
                        v_y = K.do("vector", lambda E, _tc=tcq, _tcg=tcg, _dh=dh, _e=e:
                                   E.scalar_tensor_tensor(
                            y_sb[:, 1024 * _tcg + 512 * _dh:1024 * _tcg + 512 * _dh + 512],
                            ps[:, 512 * _tc:512 * _tc + 512],
                            Pn[:, 8 * _tcg + _e:8 * _tcg + _e + 1],
                            y_sb[:, 1024 * _tcg + 512 * _dh:1024 * _tcg + 512 * _dh + 512],
                            op0=ALU.mult, op1=ALU.add), inc="sve")
                        K.release_bank(tcq, "sve", v_y)
        if debug:
            K.wait("gpsimd", "sve", K.cnt["sve"])
            K.dma("gpsimd", dbg_y, y_sb[:], fam="fo")
            K.wait("gpsimd", "sse", K.cnt["sse"])
            K.dma("gpsimd", dbg_h, h_sb[:], fam="fo")
        K.flush()
        b_st.close()

        if upto == "B":
            K.final_wait(); K.flush(); K.simulate(); st.close()
            return nc

    K.barrier()
    de_st = ExitStack()
    dsb = lambda n, sh, dt: de_st.enter_context(nc.sbuf_tensor(n, sh, dt))
    woT = dsb("woT", [128, NJ * Dm], BF16)
    woT3 = woT[:, :].rearrange("p (j n) -> p j n", n=Dm)
    att_t = [dsb(f"att_t{i}", [128, NJ * 512], BF16) for i in range(2)]
    s_t = [dsb(f"s_t{i}", [128, Dm], BF16) for i in range(2)]
    g_t = [dsb(f"g_t{i}", [128, Dm], F32) for i in range(2)]
    xe = [dsb(f"xe{i}", [128, Dm], F32) for i in range(3)]
    o_t = [dsb(f"o_t{i}", [128, Dm], F32) for i in range(2)]
    stgD = [dsb(f"stgD{i}", [128, Dm], F32) for i in range(2)]
    ysil = [dsb(f"ysil{i}", [128, Dm], F32) for i in range(2)] if routed else None
    ysil_free = [None, None]

    d_a0 = None
    stgD_free = [None, None]
    for r in range(NJ):
        sl = r % 2
        K.waitp("sync", stgD_free[sl])
        d_w = K.dma("sync", stgD[sl][:], wo_h[128 * r:128 * r + 128, :], fam="fs")
        d_a0 = K.dma("sync", att_t[0][:, 512 * r:512 * r + 512],
                     xat_h[128 * r:128 * r + 128, 0:512], fam="fat")
        K.waitp("tensor", d_w)
        for b in (2 * sl, 2 * sl + 1):
            K.use_bank("tensor", b)
        p_t = None
        for j in range(NJ):
            p_t = K.do("tensor", lambda E, _j=j, _sl=sl: E.transpose(
                ps[:, 1024 * _sl + 128 * _j:1024 * _sl + 128 * _j + 128],
                stgD[_sl][:, 128 * _j:128 * _j + 128], ident[:]),
                inc="spe" if j == NJ - 1 else None)
        stgD_free[sl] = ("spe", p_t)
        K.wait("vector", "spe", p_t)
        v_e = K.do("vector", lambda E, _r=r, _sl=sl: E.tensor_copy(
            woT3[:, :, 128 * _r:128 * _r + 128], ps3[:, 8 * _sl:8 * _sl + 8, :]),
            inc="sve")
        for b in (2 * sl, 2 * sl + 1):
            K.release_bank(b, "sve", v_e)

    K.wait("tensor", "sve", v_e)
    att_free = [None, None]
    xe_free = [None, None, None]
    g_free = [None, None]
    o_free = [None, None]
    for qt in range(NJ):
        sl = qt % 2
        if qt == 0:
            d_a = d_a0
        else:
            K.waitp("sync", att_free[sl])
            d_a = None
            for c in range(NJ):
                d_a = K.dma("sync", att_t[sl][:, 512 * c:512 * c + 512],
                            xat_h[128 * c:128 * c + 128,
                                  512 * qt:512 * qt + 512],
                            fam="fat")
        last_de_mm = None
        for tc4 in range(4):
            tcg = 4 * qt + tc4
            tsl = tc4 % 2
            xi = tcg % 3
            bb = 2 * tsl
            p_hi = None
            for ih in range(2):
                b = bb + ih
                K.use_bank("tensor", b)
                K.waitp("tensor", d_a)
                for c in range(NJ):
                    p_hi = K.do("tensor", lambda E, _c=c, _b=b, _ih=ih, _sl=sl,
                                _tc4=tc4: E.matmul(
                        ps[:, 512 * _b:512 * _b + 512],
                        att_t[_sl][:, 512 * _c + 128 * _tc4:512 * _c + 128 * _tc4 + 128],
                        woT3[:, _c, 512 * _ih:512 * _ih + 512],
                        start=(_c == 0), stop=(_c == NJ - 1)),
                        inc="spe" if c == NJ - 1 else None) or p_hi
            last_de_mm = p_hi
            K.wait("vector", "spe", p_hi)
            for ih in range(2):
                v_s = K.do("vector", lambda E, _b=bb + ih, _ih=ih, _tsl=tsl:
                           E.tensor_tensor(
                    s_t[_tsl][:, 512 * _ih:512 * _ih + 512],
                    ps[:, 512 * _b:512 * _b + 512],
                    thro[:, 512 * _ih:512 * _ih + 512], ALU.is_ge), inc="sve")
                K.release_bank(bb + ih, "sve", v_s)
            K.waitp("scalar", g_free[tsl])
            if routed:
                K.waitp("sync", ysil_free[tsl])
                d_ys = K.dma("sync", ysil[tsl][:], y_h[128 * tcg:128 * tcg + 128, :],
                             fam="fy")
                K.waitp("scalar", d_ys)
                s_si = K.do_drained("scalar", lambda E, _tsl=tsl: E.activation(
                    g_t[_tsl][:], ysil[_tsl][:], AF.Silu), inc="sse")
                ysil_free[tsl] = ("sse", s_si)
            else:
                s_si = K.do_drained("scalar", lambda E, _tcg=tcg, _tsl=tsl: E.activation(
                    g_t[_tsl][:], y_sb[:, 1024 * _tcg:1024 * _tcg + 1024], AF.Silu),
                    inc="sse")
            K.waitp("sync", xe_free[xi])
            d_x = K.dma("sync", xe[xi][:], x_h[128 * tcg:128 * tcg + 128, :], fam="fx")
            K.wait("vector", "sse", s_si)
            K.waitp("vector", d_x)
            K.waitp("vector", o_free[tsl])
            K.drain("vector")
            K.do("vector", lambda E, _tsl=tsl: E.tensor_tensor(
                g_t[_tsl][:], g_t[_tsl][:], s_t[_tsl][:], ALU.mult))
            K.drain("vector")
            v_cmb = K.do_drained("vector", lambda E, _tsl=tsl, _xi=xi: E.tensor_tensor(
                o_t[_tsl][:], g_t[_tsl][:], xe[_xi][:], ALU.add), inc="sve")
            g_free[tsl] = ("sve", v_cmb)
            xe_free[xi] = ("sve", v_cmb)
            K.wait("gpsimd", "sve", v_cmb)
            out_wr = K.dma("gpsimd", out_h[128 * tcg:128 * tcg + 128, :], o_t[tsl][:], fam="fo")
            o_free[tsl] = out_wr
            if debug:
                K.dma("gpsimd", dbg_s[128 * tcg:128 * tcg + 128, :], s_t[tsl][:], fam="fo")
        att_free[sl] = ("spe", last_de_mm)
    K.final_wait()
    K.flush()
    de_st.close()
    K.simulate()
    st.close()
    return nc


_CACHED = {}


def _get_program(upto="DE", debug=False):
    key = (upto, debug)
    if key not in _CACHED:
        _CACHED[key] = build_program(upto, debug)
    return _CACHED[key]


def make_in_maps(inputs):
    f = lambda k: np.ascontiguousarray(np.asarray(inputs[k], dtype=np.float32))
    x = f("L0_attn_x")
    base = {
        "wq": f("L0_attn_Wq"), "wk": f("L0_attn_Wk"), "wv": f("L0_attn_Wv"),
        "wo": f("L0_attn_Wo"),
        "bq": f("L0_attn_bq").reshape(8, 128), "bk": f("L0_attn_bk").reshape(8, 128),
        "bv": f("L0_attn_bv").reshape(1, 1024), "bo": f("L0_attn_bo").reshape(1, 1024),
        "wg": f("L0_ffn_Wg"), "w1": f("L0_ffn_W1"), "b1": f("L0_ffn_b1"),
        "w2": f("L0_ffn_W2"), "b2": f("L0_ffn_b2"),
    }
    return [dict(base, x=np.ascontiguousarray(x[i])) for i in range(8)]


def kernel(**inputs):
    nc = _get_program()
    in_maps = make_in_maps(inputs)
    res = run_bass_kernel_spmd(nc, in_maps, core_ids=list(range(8)))
    return np.stack([np.asarray(res.results[i]["out"]) for i in range(8)], axis=0)


# version 36
# speedup vs baseline: 1.2020x; 1.1060x over previous
import numpy as np
import ml_dtypes
from contextlib import ExitStack
import concourse.bass as bass
import concourse.mybir as mybir
from concourse.bass_utils import run_bass_kernel_spmd

F32 = mybir.dt.float32
BF16 = mybir.dt.bfloat16
AF = mybir.ActivationFunctionType
ALU = mybir.AluOpType
AX = mybir.AxisListType

T = 4096
Dm = 1024
NJ = 8
NT = 32
E8 = 8
FF = 4096
NF = 32
HP = 8
TTB = 1024
NTT = 4
CCAP = 1536
ESTRIDE = 2048
DUMPROW = 8 * ESTRIDE
IDXROWS = 16512

ENGINES = ("tensor", "scalar", "vector", "sync", "gpsimd")
DMA_FAMS = ("fb", "fx", "fxo", "fxt", "fs", "fc1", "fc2", "fcx", "fcy", "fas", "fxn",
            "fw1", "fw2", "fat", "fo", "fix", "fiv", "fga", "fys", "fyw", "fy")
SEMS = ("spe", "sse", "sve", "ssp", "spo") + DMA_FAMS
ESEM = {"tensor": "spe", "scalar": "sse", "vector": "sve", "sync": "ssp", "gpsimd": "spo"}


class KB:

    def __init__(self, nc, block, sems):
        self.nc, self.block, self.sem = nc, block, sems
        self.cnt = {s: 0 for s in SEMS}
        self.hw = {e: {s: 0 for s in SEMS} for e in ENGINES}
        self.bank_free = {b: None for b in range(8)}
        self.q = {e: [] for e in ENGINES}
        self.trace = {e: [] for e in ENGINES}

    def wait(self, eng, sem, val):
        if val is None or val <= 0 or self.hw[eng][sem] >= val:
            return
        self.hw[eng][sem] = val
        h = self.sem[sem]
        self.q[eng].append(lambda E, _h=h, _v=val: E.wait_ge(_h, _v))
        self.trace[eng].append((((sem, val),), ()))

    def waitp(self, eng, pair):
        if pair is not None:
            self.wait(eng, pair[0], pair[1])

    def do(self, eng, fn, inc=None, inc_by=1):
        if inc is None:
            self.q[eng].append(fn)
            self.trace[eng].append(((), ()))
            return None
        h = self.sem[inc]
        self.q[eng].append(lambda E, _fn=fn, _h=h, _k=inc_by: _fn(E).then_inc(_h, _k))
        self.cnt[inc] += inc_by
        v = self.cnt[inc]
        self.trace[eng].append(((), ((inc, v),)))
        self.hw[eng][inc] = max(self.hw[eng][inc], v)
        return v

    def dma(self, eng, out_ap, in_ap, fam="fb"):
        h = self.sem[fam]
        self.q[eng].append(lambda E, _o=out_ap, _i=in_ap, _h=h:
                           E.dma_start(_o, _i).then_inc(_h, 16))
        self.cnt[fam] += 16
        v = self.cnt[fam]
        self.trace[eng].append(((), ((fam, v),)))
        return (fam, v)

    def drain(self, eng):
        self.q[eng].append(lambda E: E.drain())
        self.trace[eng].append(((), ()))

    def dma_indirect_out(self, out_ap, idx_ap, in_ap, fam, op=None):
        h = self.sem[fam]
        alu = op if op is not None else ALU.bypass
        self.q["gpsimd"].append(
            lambda E, _o=out_ap, _x=idx_ap, _i=in_ap, _h=h, _a=alu:
            E.indirect_dma_start(_o, bass.IndirectOffsetOnAxis(_x, 0), _i, None,
                                 oob_is_err=False, compute_op=_a).then_inc(_h, 16))
        self.cnt[fam] += 16
        v = self.cnt[fam]
        self.trace["gpsimd"].append(((), ((fam, v),)))
        return (fam, v)

    def dma_indirect_in(self, out_ap, in_ap, idx_ap, fam):
        h = self.sem[fam]
        self.q["gpsimd"].append(
            lambda E, _o=out_ap, _x=idx_ap, _i=in_ap, _h=h:
            E.indirect_dma_start(_o, None, _i, bass.IndirectOffsetOnAxis(_x, 0),
                                 oob_is_err=False).then_inc(_h, 16))
        self.cnt[fam] += 16
        v = self.cnt[fam]
        self.trace["gpsimd"].append(((), ((fam, v),)))
        return (fam, v)

    def do_drained(self, eng, fn, inc):
        self.do(eng, fn)
        return self.do(eng, lambda E: E.drain(), inc=inc)

    def use_bank(self, eng, b):
        self.waitp(eng, self.bank_free[b])

    def release_bank(self, b, sem, val):
        self.bank_free[b] = (sem, val)

    def barrier(self):
        marks = []
        for e in ENGINES:
            s = ESEM[e]
            if e == "tensor":
                marks.append((s, self.cnt[s]))
            else:
                marks.append((s, self.do(e, lambda E: E.nop(), inc=s)))
        targets = marks + [(f, self.cnt[f]) for f in DMA_FAMS]
        for e in ENGINES:
            for s, v in targets:
                self.wait(e, s, v)

    def flush(self):
        for e in ENGINES:
            items = self.q[e]
            if not items:
                continue
            fn = getattr(self.block, {"tensor": "tensor", "scalar": "scalar",
                                      "vector": "vector", "sync": "sync",
                                      "gpsimd": "gpsimd"}[e])

            def run(E, _items=items):
                for f in _items:
                    f(E)
            fn(run)
            self.q[e] = []

    def simulate(self):
        ptr = {e: 0 for e in ENGINES}
        val = {s: 0 for s in SEMS}
        progress = True
        while progress:
            progress = False
            for e in ENGINES:
                while ptr[e] < len(self.trace[e]):
                    waits, incs = self.trace[e][ptr[e]]
                    if any(val[s] < v for s, v in waits):
                        break
                    for s, v in incs:
                        val[s] = max(val[s], v)
                    ptr[e] += 1
                    progress = True
        stuck = {e: (ptr[e], len(self.trace[e]), self.trace[e][ptr[e]][0])
                 for e in ENGINES if ptr[e] < len(self.trace[e])}
        if stuck:
            raise RuntimeError(f"build-time deadlock: {stuck} semvals={val}")

    def final_wait(self):
        for f in DMA_FAMS:
            self.hw["gpsimd"][f] = 0
            self.wait("gpsimd", f, self.cnt[f])


def build_program(upto="DE", debug=False, b_experts=E8, routed=False):
    nc = bass.Bass("TRN2", target_bir_lowering=False)

    x_h = nc.dram_tensor("x", [T, Dm], F32, kind="ExternalInput").ap()
    wq_h = nc.dram_tensor("wq", [Dm, Dm], F32, kind="ExternalInput").ap()
    wk_h = nc.dram_tensor("wk", [Dm, Dm], F32, kind="ExternalInput").ap()
    wv_h = nc.dram_tensor("wv", [Dm, Dm], F32, kind="ExternalInput").ap()
    wo_h = nc.dram_tensor("wo", [Dm, Dm], F32, kind="ExternalInput").ap()
    bq_h = nc.dram_tensor("bq", [8, 128], F32, kind="ExternalInput").ap()
    bk_h = nc.dram_tensor("bk", [8, 128], F32, kind="ExternalInput").ap()
    bv_h = nc.dram_tensor("bv", [1, Dm], F32, kind="ExternalInput").ap()
    bo_h = nc.dram_tensor("bo", [1, Dm], F32, kind="ExternalInput").ap()
    wg_h = nc.dram_tensor("wg", [Dm, 8], F32, kind="ExternalInput").ap()
    w1_h = nc.dram_tensor("w1", [E8, Dm, FF], F32, kind="ExternalInput").ap()
    b1_h = nc.dram_tensor("b1", [E8, FF], F32, kind="ExternalInput").ap()
    w2_h = nc.dram_tensor("w2", [E8, FF, Dm], F32, kind="ExternalInput").ap()
    b2_h = nc.dram_tensor("b2", [E8, Dm], F32, kind="ExternalInput").ap()
    out_h = nc.dram_tensor("out", [T, Dm], F32, kind="ExternalOutput").ap()

    skind = "ExternalOutput" if debug else "Internal"
    xnt_h = nc.dram_tensor("dbg_xnt" if debug else "xnt", [Dm, T], BF16, kind=skind).ap()
    xat_h = nc.dram_tensor("dbg_xat" if debug else "xat", [Dm, T], BF16, kind=skind).ap()
    w1b_h = nc.dram_tensor("w1b", [E8, NF, 128, NJ, 128], BF16, kind="Internal").ap()
    xtok_h = nc.dram_tensor("xtok", [T, Dm], F32, kind="Internal").ap()
    idxP_h = nc.dram_tensor("idxP", [16640, 2], F32, kind="Internal").ap()
    y_h = nc.dram_tensor("dbg_yh" if debug else "yh", [T, Dm], F32,
                         kind="ExternalOutput" if debug else "Internal").ap()
    w2b_h = nc.dram_tensor("w2b", [E8, 2, NF, 128, 512], BF16, kind="Internal").ap()
    if debug:
        dbg_pn = nc.dram_tensor("dbg_pn", [128, 256], F32, kind="ExternalOutput").ap()
        dbg_y = nc.dram_tensor("dbg_y", [128, NT * Dm], BF16, kind="ExternalOutput").ap()
        dbg_s = nc.dram_tensor("dbg_s", [T, Dm], BF16, kind="ExternalOutput").ap()
        dbg_h = nc.dram_tensor("dbg_h", [128, NF * TTB], BF16, kind="ExternalOutput").ap()

    st = ExitStack()
    sems = {s: st.enter_context(nc.semaphore(s)) for s in SEMS}
    sb = lambda name, shape, dt: st.enter_context(nc.sbuf_tensor(name, shape, dt))

    ident = sb("ident", [128, 128], F32)
    ident1 = sb("ident1", [1, 1], F32)
    ones1 = sb("ones1", [1, 128], F32)
    wg_bf = sb("wg_bf", [128, 64], BF16)
    thrq = sb("thrq", [128, 8], F32)
    thrk = sb("thrk", [128, 8], F32)
    bvb = sb("bvb", [128, Dm], F32)
    thro = sb("thro", [128, Dm], F32)
    b1c = sb("b1c", [128, 256], F32)
    b2bf = sb("b2bf", [8, Dm], BF16)
    Pn = sb("Pn", [128, 256], F32)
    PnT = sb("PnT", [8, T], BF16)
    ps = st.enter_context(nc.psum_tensor("ps", [128, 4096], F32))
    ps3 = ps[:, :].rearrange("p (g c) -> p g c", c=128)

    block = st.enter_context(nc.Block())
    K = KB(nc, block, sems)

    init_st = ExitStack()
    isb = lambda n, sh, dt: init_st.enter_context(nc.sbuf_tensor(n, sh, dt))
    bqn = isb("bqn", [8, 128], F32)
    bkn = isb("bkn", [8, 128], F32)
    bvr = isb("bvr", [1, Dm], F32)
    bor = isb("bor", [1, Dm], F32)
    b1n = isb("b1n", [8, FF], F32)
    b2f = isb("b2f", [8, Dm], F32)
    wgf = isb("wgf", [128, 64], F32)

    K.do("gpsimd", lambda E: E.memset(ident[:], 0.0))
    K.drain("gpsimd")
    K.do("gpsimd", lambda E: E.affine_select(out=ident[:], in_=ident[:],
                                             compare_op=ALU.not_equal, fill=1.0,
                                             base=0, pattern=[[-1, 128]],
                                             channel_multiplier=1))
    K.do("gpsimd", lambda E: E.memset(ident1[:], 1.0))
    po_ready = K.do("gpsimd", lambda E: E.memset(ones1[:], 1.0), inc="spo")

    K.dma("sync", bkn[:], bk_h)
    K.dma("sync", bqn[:], bq_h)
    K.dma("sync", bvr[:], bv_h)
    K.dma("sync", bor[:], bo_h)
    K.dma("sync", b2f[:], b2_h)
    K.dma("sync", b1n[:], b1_h)
    for c in range(8):
        K.dma("sync", wgf[:, 8 * c:8 * c + 8], wg_h[128 * c:128 * c + 128, :])
    d_init = ("fb", K.cnt["fb"])

    K.waitp("vector", d_init)
    v_wg = K.do("vector", lambda E: E.tensor_copy(wg_bf[:], wgf[:]), inc="sve")
    v_b2 = K.do("vector", lambda E: E.tensor_copy(b2bf[:], b2f[:]), inc="sve")
    v_bo = K.do("vector", lambda E: E.tensor_scalar(bor[:], bor[:], 1.0, -1.0,
                                                    ALU.subtract, ALU.mult), inc="sve")

    K.wait("tensor", "spo", po_ready)
    K.waitp("tensor", d_init)
    K.do("tensor", lambda E: E.transpose(ps[:, 0:8], bqn[:], ident[0:8, 0:8]))
    p_bk = K.do("tensor", lambda E: E.transpose(ps[:, 8:16], bkn[:], ident[0:8, 0:8]),
                inc="spe")
    K.waitp("tensor", d_init)
    p_b1 = None
    for fc in range(NF):
        p_b1 = K.do("tensor", lambda E, _fc=fc: E.transpose(
            ps[:, 512 + 8 * _fc:520 + 8 * _fc],
            b1n[0:8, 128 * _fc:128 * _fc + 128], ident[0:8, 0:8]),
            inc="spe" if fc == NF - 1 else None)
    K.waitp("tensor", d_init)
    K.do("tensor", lambda E: E.matmul(ps[:, 1024:1536], ones1[:], bvr[0:1, 0:512],
                                      start=True, stop=True))
    p_bv = K.do("tensor", lambda E: E.matmul(ps[:, 1536:2048], ones1[:], bvr[0:1, 512:1024],
                                             start=True, stop=True), inc="spe")
    K.wait("tensor", "sve", v_bo)
    K.do("tensor", lambda E: E.matmul(ps[:, 2048:2560], ones1[:], bor[0:1, 0:512],
                                      start=True, stop=True))
    p_bo = K.do("tensor", lambda E: E.matmul(ps[:, 2560:3072], ones1[:], bor[0:1, 512:1024],
                                             start=True, stop=True), inc="spe")

    K.wait("vector", "spe", p_bk)
    K.do("vector", lambda E: E.tensor_scalar(thrq[:], ps[:, 0:8], 1.0, -1.0,
                                             ALU.subtract, ALU.mult))
    K.do("vector", lambda E: E.tensor_scalar(thrk[:], ps[:, 8:16], 1.0, -1.0,
                                             ALU.subtract, ALU.mult))
    K.wait("vector", "spe", p_b1)
    K.do("vector", lambda E: E.tensor_copy(b1c[:], ps[:, 512:768]))
    K.wait("vector", "spe", p_bv)
    K.do("vector", lambda E: E.tensor_copy(bvb[:], ps[:, 1024:2048]))
    K.wait("vector", "spe", p_bo)
    v_init = K.do("vector", lambda E: E.tensor_copy(thro[:], ps[:, 2048:3072]), inc="sve")
    for b in range(6):
        K.release_bank(b, "sve", v_init)
    K.flush()
    init_st.close()

    r2_st = ExitStack()
    r2_sb = lambda n, sh, dt: r2_st.enter_context(nc.sbuf_tensor(n, sh, dt))
    if routed:
        off32 = r2_sb("off32", [128, 256], mybir.dt.int32)
        tok32 = r2_sb("tok32", [128, 32], F32)
        vals = r2_sb("vals", [128, 512], F32)
        zero_sb = r2_sb("zero_sb", [128, 260], F32)
    xnT_st = ExitStack()
    xnT = xnT_st.enter_context(nc.sbuf_tensor("xnT", [128, NJ * T], BF16))
    xnT3 = xnT[:, :].rearrange("p (j n) -> p j n", n=T)

    a_st = ExitStack()
    asb = lambda n, sh, dt: a_st.enter_context(nc.sbuf_tensor(n, sh, dt))
    x_sb = [asb(f"x_sb{i}", [128, Dm], F32) for i in range(3)]
    xnf = [asb(f"xnf{i}", [128, Dm], F32) for i in range(3)]
    sq_scr = asb("sq_scr", [128, Dm], F32)
    cols = asb("cols", [128, 24], F32)

    x_free = [None, None, None]
    xnf_free = [None, None, None]
    evac_v = None
    for t in range(NT):
        sl = t % 3
        if x_free[sl] is not None:
            for pr in x_free[sl]:
                K.waitp("sync", pr)
        d_x = K.dma("sync", x_sb[sl][:], x_h[128 * t:128 * t + 128, :], fam="fx")
        co = 8 * sl
        c_xsum, c_sumsq = cols[:, co:co + 1], cols[:, co + 1:co + 2]
        c_mean, c_ex2 = cols[:, co + 2:co + 3], cols[:, co + 3:co + 4]
        c_var, c_std = cols[:, co + 4:co + 5], cols[:, co + 5:co + 6]
        c_rstd, c_nmean = cols[:, co + 6:co + 7], cols[:, co + 7:co + 8]

        K.waitp("vector", d_x)
        K.do("vector", lambda E, a=c_xsum, b=x_sb[sl]: E.reduce_sum(a, b[:], axis=AX.X))
        K.waitp("scalar", d_x)
        s_sq = K.do_drained("scalar", lambda E, b=x_sb[sl], a=c_sumsq: E.activation(
            sq_scr[:], b[:], AF.Square, accum_out=a), inc="sse")
        K.drain("vector")
        K.do("vector", lambda E, a=c_mean, b=c_xsum: E.tensor_scalar(
            a, b, 1.0 / Dm, None, ALU.mult))
        K.do("vector", lambda E, a=c_nmean, b=c_xsum: E.tensor_scalar(
            a, b, -1.0 / Dm, None, ALU.mult))
        K.wait("vector", "sse", s_sq)
        K.do("vector", lambda E, a=c_ex2, b=c_sumsq: E.tensor_scalar(
            a, b, 1.0 / Dm, 1e-5, ALU.mult, ALU.add))
        K.drain("vector")
        v_var = K.do_drained("vector", lambda E, nm=c_nmean, m=c_mean, a=c_ex2, v=c_var:
                     E.scalar_tensor_tensor(v, nm, m, a, op0=ALU.mult, op1=ALU.add),
                     inc="sve")
        K.wait("scalar", "sve", v_var)
        s_std = K.do_drained("scalar", lambda E, v=c_var, s_=c_std: E.activation(
            s_, v, AF.Sqrt), inc="sse")
        K.wait("vector", "sse", s_std)
        K.do("vector", lambda E, s_=c_std, r=c_rstd: E.reciprocal(r, s_))
        if xnf_free[sl] is not None:
            for pr in xnf_free[sl]:
                K.waitp("vector", pr)
        K.drain("vector")
        v_xn = K.do_drained("vector", lambda E, xx=x_sb[sl], m=c_mean, r=c_rstd, o=xnf[sl]:
                    E.tensor_scalar(o[:], xx[:], m, r, ALU.subtract, ALU.mult), inc="sve")
        x_free[sl] = (("sve", v_xn), ("sse", s_sq))
        if routed:
            K.wait("gpsimd", "sve", v_xn)
            d_xt = K.dma("gpsimd", xtok_h[128 * t:128 * t + 128, :], xnf[sl][:], fam="fxt")
        else:
            d_xt = None

        K.wait("tensor", "sve", v_xn)
        for b in (2 * sl, 2 * sl + 1):
            K.use_bank("tensor", b)
        p_tr = None
        for j in range(NJ):
            p_tr = K.do("tensor", lambda E, _j=j, _sl=sl, _x=xnf[sl]: E.transpose(
                ps[:, 1024 * _sl + 128 * _j:1024 * _sl + 128 * _j + 128],
                _x[:, 128 * _j:128 * _j + 128], ident[:]),
                inc="spe" if j == NJ - 1 else None)
        xnf_free[sl] = (("spe", p_tr), d_xt) if d_xt else (("spe", p_tr),)
        K.wait("vector", "spe", p_tr)
        evac_v = K.do("vector", lambda E, _t=t, _sl=sl: E.tensor_copy(
            xnT3[:, :, 128 * _t:128 * _t + 128], ps3[:, 8 * _sl:8 * _sl + 8, :]), inc="sve")
        for b in (2 * sl, 2 * sl + 1):
            K.release_bank(b, "sve", evac_v)

    evac_v = K.do("vector", lambda E: E.drain(), inc="sve")
    xnt_wr = None
    K.wait("gpsimd", "sve", evac_v)
    for j in range(NJ):
        xnt_wr = K.dma("gpsimd", xnt_h[128 * j:128 * j + 128, :], xnT[:, T * j:T * j + T], fam="fxo")
    K.flush()
    a_st.close()

    r_st = ExitStack()
    PnTf = r_st.enter_context(nc.sbuf_tensor("PnTf", [8, T], F32)) if routed else None
    a2_st = ExitStack()
    Eex = [a2_st.enter_context(nc.sbuf_tensor(f"Eex{i}", [128, 8], F32)) for i in range(2)]
    g8 = [a2_st.enter_context(nc.sbuf_tensor(f"g8_{i}", [128, 8], F32)) for i in range(2)]
    gc = a2_st.enter_context(nc.sbuf_tensor("gc", [128, 8], F32))

    K.wait("tensor", "sve", evac_v)
    K.wait("tensor", "sve", v_wg)
    eex_free = [None, None]
    pn_last = None
    for t in range(NT):
        sl = t % 2
        b_g = 4 + sl
        K.use_bank("tensor", b_g)
        p_g = None
        for j in range(NJ):
            p_g = K.do("tensor", lambda E, _j=j, _b=b_g, _t=t: E.matmul(
                ps[:, 512 * _b:512 * _b + 8],
                xnT3[:, _j, 128 * _t:128 * _t + 128],
                wg_bf[:, 8 * _j:8 * _j + 8],
                start=(_j == 0), stop=(_j == NJ - 1)),
                inc="spe" if j == NJ - 1 else None)
        co = 4 * sl
        c_m1, c_m2 = gc[:, co:co + 1], gc[:, co + 1:co + 2]
        c_den, c_s = gc[:, co + 2:co + 3], gc[:, co + 3:co + 4]
        K.wait("vector", "spe", p_g)
        v_m1 = K.do_drained("vector", lambda E, _b=b_g, m=c_m1: E.reduce_max(
            m, ps[:, 512 * _b:512 * _b + 8], axis=AX.X, negate=True), inc="sve")
        K.wait("scalar", "sve", v_m1)
        K.waitp("scalar", eex_free[sl])
        s_e = K.do_drained("scalar", lambda E, _b=b_g, m=c_m1, o=Eex[sl]: E.activation(
            o[:], ps[:, 512 * _b:512 * _b + 8], AF.Exp, bias=m), inc="sse")
        K.release_bank(b_g, "sse", s_e)
        ex, g = Eex[sl], g8[sl]
        K.wait("vector", "sse", s_e)
        K.do("vector", lambda E, ex=ex, g=g: E.tensor_scalar(
            g[:], ex[:], 1.0, -1e30, ALU.is_ge, ALU.mult))
        K.do("vector", lambda E, ex=ex, s_=c_s: E.reduce_sum(s_, ex[:], axis=AX.X))
        K.drain("vector")
        K.do("vector", lambda E, ex=ex, g=g: E.tensor_tensor(g[:], g[:], ex[:], ALU.add))
        K.drain("vector")
        K.do("vector", lambda E, g=g, m=c_m2: E.reduce_max(m, g[:], axis=AX.X))
        K.drain("vector")
        K.do("vector", lambda E, ex=ex, g=g, m=c_m2: E.tensor_scalar(
            g[:], ex[:], m, None, ALU.is_ge))
        K.drain("vector")
        K.do("vector", lambda E, ex=ex, g=g: E.tensor_tensor(g[:], g[:], ex[:], ALU.mult))
        K.drain("vector")
        K.do("vector", lambda E, g=g, d=c_den: E.reduce_sum(d, g[:], axis=AX.X))
        K.drain("vector")
        K.do("vector", lambda E, s_=c_s, d=c_den: E.scalar_tensor_tensor(
            d, s_, 1e-9, d, op0=ALU.mult, op1=ALU.add))
        K.drain("vector")
        K.do("vector", lambda E, d=c_den: E.reciprocal(d, d))
        K.drain("vector")
        v_pn = K.do_drained("vector", lambda E, g=g, d=c_den, _t=t: E.tensor_scalar(
            Pn[:, 8 * _t:8 * _t + 8], g[:], d, None, ALU.mult), inc="sve")
        eex_free[sl] = ("sve", v_pn)
        b_t = 6 + sl
        K.use_bank("tensor", b_t)
        K.wait("tensor", "sve", v_pn)
        p_pt = K.do("tensor", lambda E, _b=b_t, _t=t: E.transpose(
            ps[0:8, 512 * _b:512 * _b + 128], Pn[:, 8 * _t:8 * _t + 8], ident[:]), inc="spe")
        K.wait("vector", "spe", p_pt)
        if routed:
            K.do("vector", lambda E, _b=b_t, _t=t: E.tensor_copy(
                PnT[0:8, 128 * _t:128 * _t + 128], ps[0:8, 512 * _b:512 * _b + 128]))
            pn_last = K.do("vector", lambda E, _b=b_t, _t=t: E.tensor_copy(
                PnTf[0:8, 128 * _t:128 * _t + 128], ps[0:8, 512 * _b:512 * _b + 128]),
                inc="sve")
        else:
            pn_last = K.do("vector", lambda E, _b=b_t, _t=t: E.tensor_copy(
                PnT[0:8, 128 * _t:128 * _t + 128], ps[0:8, 512 * _b:512 * _b + 128]),
                inc="sve")
        K.release_bank(b_t, "sve", pn_last)
    if debug:
        K.wait("gpsimd", "sve", pn_last)
        K.dma("gpsimd", dbg_pn, Pn[:], fam="fo")
    K.flush()
    a2_st.close()

    if upto == "A2":
        K.final_wait(); K.flush(); K.simulate()
        r_st.close(); r2_st.close(); xnT_st.close(); st.close()
        return nc

    fix_total = 0
    if routed:
        rsb = lambda n, sh, dt: r_st.enter_context(nc.sbuf_tensor(n, sh, dt))
        maskT = rsb("maskT", [8, T], F32)
        posT = rsb("posT", [8, T], F32)
        offT = rsb("offT", [8, T], F32)
        vldT = rsb("vldT", [8, T], F32)
        eoff = rsb("eoff", [8, T], F32)

        po_r = K.do("gpsimd", lambda E: E.iota(
            eoff[:], [[0, T]], channel_multiplier=ESTRIDE,
            allow_small_or_imprecise_dtypes=True), inc="spo")
        K.do("gpsimd", lambda E: E.memset(zero_sb[:], 0.0))
        po_r = K.do("gpsimd", lambda E: E.iota(
            tok32[:], [[128, 32]], channel_multiplier=1,
            allow_small_or_imprecise_dtypes=True), inc="spo")
        d_z1 = K.dma("gpsimd", idxP_h.rearrange("(a b) c -> a (b c)", b=130),
                     zero_sb[:], fam="fix")

        K.wait("vector", "sve", pn_last)
        K.drain("vector")
        K.do("vector", lambda E: E.tensor_scalar(maskT[:], PnTf[:], 0.0, None, ALU.is_gt))
        K.drain("vector")
        K.do("vector", lambda E: E.tensor_tensor_scan(
            posT[:], maskT[:], maskT[:], 0.0, op0=ALU.add, op1=ALU.bypass))
        K.drain("vector")
        K.do("vector", lambda E: E.tensor_tensor(posT[:], posT[:], maskT[:], ALU.subtract))
        K.do("vector", lambda E: E.tensor_scalar(vldT[:], posT[:], float(CCAP), None,
                                                 ALU.is_lt))
        K.drain("vector")
        K.do("vector", lambda E: E.tensor_tensor(maskT[:], maskT[:], vldT[:], ALU.mult))
        K.wait("vector", "spo", po_r)
        K.do("vector", lambda E: E.tensor_tensor(offT[:], posT[:], eoff[:], ALU.add))
        K.drain("vector")
        K.do("vector", lambda E: E.tensor_scalar(offT[:], offT[:], -float(DUMPROW), None,
                                                 ALU.add))
        K.drain("vector")
        K.do("vector", lambda E: E.tensor_tensor(offT[:], offT[:], maskT[:], ALU.mult))
        K.drain("vector")
        v_off = K.do_drained("vector", lambda E: E.tensor_scalar(
            offT[:], offT[:], float(DUMPROW), None, ALU.add), inc="sve")

        for c in range(NT):
            for e in range(E8):
                k = 8 * c + e
                K.do("vector", lambda E, _c=c, _k=k: E.tensor_copy(
                    vals[:, 2 * _k:2 * _k + 1], tok32[:, _c:_c + 1]))
                K.do("vector", lambda E, _k=k: E.tensor_copy(
                    vals[:, 2 * _k + 1:2 * _k + 2], Pn[:, _k:_k + 1]))
        K.drain("vector")

        K.wait("tensor", "sve", v_off)
        v_o32 = None
        for c in range(NT):
            b = c % 2
            K.use_bank("tensor", b)
            p_ot = K.do("tensor", lambda E, _b=b, _c=c: E.transpose(
                ps[:, 512 * _b:512 * _b + 8], offT[0:8, 128 * _c:128 * _c + 128],
                ident[0:8, 0:8]), inc="spe")
            K.wait("vector", "spe", p_ot)
            v_o32 = K.do("vector", lambda E, _b=b, _c=c: E.tensor_copy(
                off32[:, 8 * _c:8 * _c + 8], ps[:, 512 * _b:512 * _b + 8]), inc="sve")
            K.release_bank(b, "sve", v_o32)
        v_o32 = K.do("vector", lambda E: E.drain(), inc="sve")

        K.wait("gpsimd", "sve", v_o32)
        K.waitp("gpsimd", d_z1)
        for c in range(NT):
            for e in range(E8):
                k = 8 * c + e
                K.dma_indirect_out(idxP_h, off32[:, k:k + 1],
                                   vals[:, 2 * k:2 * k + 2], fam="fix")
        fix_total = K.cnt["fix"]
    K.flush()
    r_st.close()

    c_st = ExitStack()
    csb = lambda n, sh, dt: c_st.enter_context(nc.sbuf_tensor(n, sh, dt))
    wqT = csb("wqT", [128, NJ * Dm], BF16)
    wkT = csb("wkT", [128, NJ * Dm], BF16)
    wvT = csb("wvT", [128, NJ * Dm], BF16)
    wqT3 = wqT[:, :].rearrange("p (j n) -> p j n", n=Dm)
    wkT3 = wkT[:, :].rearrange("p (j n) -> p j n", n=Dm)
    wvT3 = wvT[:, :].rearrange("p (j n) -> p j n", n=Dm)
    Qs = csb("Qs", [128, T], BF16)
    Ks = csb("Ks", [128, T], BF16)
    Vt = csb("Vt", [128, T], BF16)
    rA = csb("rA", [128, 4 * 1024], BF16)
    a_stage = csb("a_stage", [128, T], BF16)
    stg_c = [csb(f"stg_c{i}", [128, 2048], F32) for i in range(3)]
    cvo = [csb(f"cvo{i}", [128, 2048], BF16) for i in range(2)]

    conv_steps = []
    for e in range(E8):
        for k in range(NJ):
            for half in range(2):
                conv_steps.append(("w1",
                                   w1_h[e, 128 * k:128 * k + 128,
                                        2048 * half:2048 * half + 2048],
                                   w1b_h[e, 16 * half:16 * half + 16, :, k, :]
                                   .rearrange("fc p f -> p fc f")))
        for fi in range(NF):
            conv_steps.append(("w2", w2_h[e, 128 * fi:128 * fi + 128, :],
                               (w2b_h[e, 0, fi], w2b_h[e, 1, fi])))
    conv_i = [0]
    stg_ctr = [0]
    cstg_free = [None, None, None]
    conv_free = [None, None, None]
    cvo_free = [None, None]

    def emit_conv(n):
        for _ in range(n):
            i = conv_i[0]
            if i >= len(conv_steps):
                return
            conv_i[0] += 1
            kind, src_, dst = conv_steps[i]
            sl = stg_ctr[0] % 3
            stg_ctr[0] += 1
            sl2 = i % 2
            nfree = 2048 if kind == "w1" else 1024
            K.waitp("sync", cstg_free[sl])
            K.waitp("sync", conv_free[sl])
            d_in = K.dma("sync", stg_c[sl][:, 0:nfree], src_,
                         fam="fc1" if kind == "w1" else "fc2")
            K.waitp("gpsimd", d_in)
            K.waitp("gpsimd", cvo_free[sl2])
            v_c = K.do("gpsimd", lambda E, _sl=sl, _s2=sl2, _n=nfree:
                       E.tensor_copy(cvo[_s2][:, 0:_n], stg_c[_sl][:, 0:_n]),
                       inc="spo")
            conv_free[sl] = ("spo", v_c)
            cstg_free[sl] = ("spo", v_c)
            if kind == "w1":
                w_ = K.dma("gpsimd", dst,
                           cvo[sl2][:, 0:2048].rearrange("p (fc f) -> p fc f", f=128),
                           fam="fcx")
            else:
                K.dma("gpsimd", dst[0], cvo[sl2][:, 0:512], fam="fcy")
                w_ = K.dma("gpsimd", dst[1], cvo[sl2][:, 512:1024], fam="fcy")
            cvo_free[sl2] = w_

    emit_conv(48)

    cprep_done = None
    for m, (w_src, wT3m) in enumerate([(wq_h, wqT3), (wk_h, wkT3), (wv_h, wvT3)]):
        for r in range(NJ):
            sl = stg_ctr[0] % 3
            stg_ctr[0] += 1
            psl = (m * NJ + r) % 2
            K.waitp("sync", cstg_free[sl])
            d_w = K.dma("sync", stg_c[sl][:, 0:1024], w_src[128 * r:128 * r + 128, :],
                        fam="fs")
            K.waitp("tensor", d_w)
            for b in (2 * psl, 2 * psl + 1):
                K.use_bank("tensor", b)
            p_t = None
            for j in range(NJ):
                p_t = K.do("tensor", lambda E, _j=j, _sl=sl, _p=psl: E.transpose(
                    ps[:, 1024 * _p + 128 * _j:1024 * _p + 128 * _j + 128],
                    stg_c[_sl][:, 128 * _j:128 * _j + 128], ident[:]),
                    inc="spe" if j == NJ - 1 else None)
            cstg_free[sl] = ("spe", p_t)
            K.wait("vector", "spe", p_t)
            v_e = K.do("vector", lambda E, _r=r, _p=psl, _w=wT3m: E.tensor_copy(
                _w[:, :, 128 * _r:128 * _r + 128], ps3[:, 8 * _p:8 * _p + 8, :]),
                inc="sve")
            cprep_done = v_e
            for b in (2 * psl, 2 * psl + 1):
                K.release_bank(b, "sve", v_e)

    stage_free = [None] * NJ
    attn_reads_done = None
    v_o = None
    for hp in range(HP):
        K.wait("tensor", "sve", cprep_done)
        if attn_reads_done is not None:
            K.wait("vector", "spe", attn_reads_done)
        for (Wt3, dst, thr_ap) in ((wqT3, Qs, thrq), (wkT3, Ks, thrk)):
            for tn in range(NJ):
                b = 4 + (tn % 2)
                K.use_bank("tensor", b)
                p_q = None
                for j in range(NJ):
                    p_q = K.do("tensor", lambda E, _j=j, _b=b, _tn=tn, _w=Wt3, _hp=hp:
                               E.matmul(
                        ps[:, 512 * _b:512 * _b + 512],
                        _w[:, _j, 128 * _hp:128 * _hp + 128],
                        xnT3[:, _j, 512 * _tn:512 * _tn + 512],
                        start=(_j == 0), stop=(_j == NJ - 1)),
                        inc="spe" if j == NJ - 1 else None)
                K.wait("vector", "spe", p_q)
                v_q = K.do("vector", lambda E, _b=b, _tn=tn, _d=dst, _t=thr_ap, _hp=hp:
                           E.tensor_scalar(_d[:, 512 * _tn:512 * _tn + 512],
                                           ps[:, 512 * _b:512 * _b + 512],
                                           _t[:, _hp:_hp + 1], None, ALU.is_ge),
                           inc="sve")
                K.release_bank(b, "sve", v_q)
        for tc in range(NT):
            b = 4 + (tc % 2)
            K.use_bank("tensor", b)
            p_v = None
            for j in range(NJ):
                p_v = K.do("tensor", lambda E, _j=j, _b=b, _tc=tc, _hp=hp: E.matmul(
                    ps[:, 512 * _b:512 * _b + 128],
                    xnT3[:, _j, 128 * _tc:128 * _tc + 128],
                    wvT3[:, _j, 128 * _hp:128 * _hp + 128],
                    start=(_j == 0), stop=(_j == NJ - 1)),
                    inc="spe" if j == NJ - 1 else None)
            K.wait("vector", "spe", p_v)
            v_v = K.do("vector", lambda E, _b=b, _tc=tc, _hp=hp: E.tensor_tensor(
                Vt[:, 128 * _tc:128 * _tc + 128], ps[:, 512 * _b:512 * _b + 128],
                bvb[:, 128 * _hp:128 * _hp + 128], ALU.add), inc="sve")
            K.release_bank(b, "sve", v_v)
        K.wait("tensor", "sve", v_v)

        av_done = {}
        sig_done = {}
        for qt in range(NJ):
            b_o = 4 + (qt % 2)
            K.use_bank("tensor", b_o)

            def emit_av(kc, _qt=None):
                q_ = qt if _qt is None else _qt
                slot = kc % 4
                K.wait("tensor", "sse", sig_done[(q_, kc)])
                K.do("tensor", lambda E, _kc=kc, _sl=slot, _b=b_o: E.matmul(
                    ps[0:64, 512 * _b:512 * _b + 512],
                    Vt[:, 128 * _kc:128 * _kc + 64],
                    rA[:, 1024 * _sl:1024 * _sl + 512],
                    start=(_kc == 0), stop=(_kc == NT - 1), tile_position=(0, 0)))
                av = K.do("tensor", lambda E, _kc=kc, _sl=slot, _b=b_o: E.matmul(
                    ps[64:128, 512 * _b:512 * _b + 512],
                    Vt[:, 128 * _kc + 64:128 * _kc + 128],
                    rA[:, 1024 * _sl + 512:1024 * _sl + 1024],
                    start=(_kc == 0), stop=(_kc == NT - 1), tile_position=(0, 64)),
                    inc="spe")
                av_done[(q_, kc)] = av

            for kc in range(NT):
                bs = 2 * (kc % 2)
                K.use_bank("tensor", bs)
                K.use_bank("tensor", bs + 1)
                K.do("tensor", lambda E, _b=bs, _kc=kc, _qt=qt: E.matmul(
                    ps[:, 512 * _b:512 * _b + 512],
                    Ks[0:64, 128 * _kc:128 * _kc + 128],
                    Qs[0:64, 512 * _qt:512 * _qt + 512],
                    start=True, stop=True, tile_position=(0, 0)))
                p_sB = K.do("tensor", lambda E, _b=bs + 1, _kc=kc, _qt=qt: E.matmul(
                    ps[:, 512 * _b:512 * _b + 512],
                    Ks[64:128, 128 * _kc:128 * _kc + 128],
                    Qs[64:128, 512 * _qt:512 * _qt + 512],
                    start=True, stop=True, tile_position=(64, 0)), inc="spe")
                slot = kc % 4
                K.wait("scalar", "spe", p_sB)
                if kc >= 4:
                    K.wait("scalar", "spe", av_done[(qt, kc - 4)])
                elif qt > 0:
                    K.wait("scalar", "spe", av_done[(qt - 1, NT - 4 + kc)])
                s_AB = K.do("scalar", lambda E, _b=bs, _sl=slot: E.activation(
                    rA[:, 1024 * _sl:1024 * _sl + 1024],
                    ps[:, 512 * _b:512 * _b + 1024],
                    AF.Sigmoid, scale=0.125), inc="sse")
                K.release_bank(bs, "sse", s_AB)
                K.release_bank(bs + 1, "sse", s_AB)
                sig_done[(qt, kc)] = s_AB
                if kc > 1:
                    emit_av(kc - 2)
            emit_av(NT - 2)
            emit_av(NT - 1)
            K.wait("vector", "spe", av_done[(qt, NT - 1)])
            K.waitp("vector", stage_free[qt])
            v_o = K.do_drained("vector", lambda E, _b=b_o, _qt=qt: E.tensor_copy(
                a_stage[:, 512 * _qt:512 * _qt + 512],
                ps[:, 512 * _b:512 * _b + 512]), inc="sve")
            K.release_bank(b_o, "sve", v_o)
            K.wait("sync", "sve", v_o)
            stage_free[qt] = K.dma(
                "sync", xat_h[128 * hp:128 * hp + 128, 512 * qt:512 * qt + 512],
                a_stage[:, 512 * qt:512 * qt + 512], fam="fas")
            emit_conv(6)
        attn_reads_done = av_done[(NJ - 1, NT - 1)]
    emit_conv(10 ** 9)
    conv_all = (("fcx", K.cnt["fcx"]), ("fcy", K.cnt["fcy"]))
    K.flush()
    c_st.close()
    xnT_st.close()

    if upto == "C":
        K.final_wait(); K.flush(); K.simulate()
        r2_st.close(); st.close()
        return nc

    K.barrier()
    if upto == "BAR":
        K.final_wait(); K.flush(); K.simulate()
        r2_st.close(); st.close()
        return nc

    if routed:
        b_st = ExitStack()
        bsb = lambda n, sh, dt: b_st.enter_context(nc.sbuf_tensor(n, sh, dt))
        xg = [bsb(f"xg{i}", [128, Dm], F32) for i in range(2)]
        xgT = [bsb(f"xgT{i}", [128, NJ * 512], BF16) for i in range(2)]
        xgT3 = [x[:, :].rearrange("p (j n) -> p j n", n=512) for x in xgT]
        h_sb = bsb("h_sb", [128, NF * 512], BF16)
        w1bf = [bsb(f"w1bf{i}", [128, 4096], BF16) for i in range(2)]
        w2bf = [bsb(f"w2bf{i}", [128, 2048], BF16) for i in range(2)]
        ysc2 = [bsb(f"ysc{i}", [128, 4 * Dm], F32) for i in range(2)]
        yb = [bsb(f"yb{i}", [128, Dm], F32) for i in range(2)]
        idxi = [bsb(f"idxi{i}", [128, 4], mybir.dt.int32) for i in range(2)]
        pcolb = [bsb(f"pcolb{i}", [128, 4], F32) for i in range(2)]
        idxf = [bsb(f"idxf{i}", [128, 2], F32) for i in range(2)]

        K.wait("tensor", "sve", pn_last)
        K.wait("tensor", "sve", v_b2)
        yb_free = [None, None]
        for tc in range(NT):
            bb = 2 * (tc % 2)
            K.use_bank("tensor", bb)
            K.use_bank("tensor", bb + 1)
            p_y0 = None
            for ih in range(2):
                p_y0 = K.do("tensor", lambda E, _tc=tc, _ih=ih, _bb=bb: E.matmul(
                    ps[:, 512 * (_bb + _ih):512 * (_bb + _ih) + 512],
                    PnT[0:8, 128 * _tc:128 * _tc + 128],
                    b2bf[0:8, 512 * _ih:512 * _ih + 512],
                    start=True, stop=True), inc="spe" if ih == 1 else None)
            K.wait("vector", "spe", p_y0)
            K.waitp("vector", yb_free[tc % 2])
            v_y0 = K.do_drained("vector", lambda E, _tc=tc, _bb=bb: E.tensor_copy(
                yb[_tc % 2][:], ps[:, 512 * _bb:512 * _bb + 1024]), inc="sve")
            K.release_bank(bb, "sve", v_y0)
            K.release_bank(bb + 1, "sve", v_y0)
            K.wait("gpsimd", "sve", v_y0)
            d_y0 = K.dma("gpsimd", y_h[128 * tc:128 * tc + 128, :], yb[tc % 2][:],
                         fam="fyw")
            yb_free[tc % 2] = d_y0
        K.wait("gpsimd", "fyw", K.cnt["fyw"])

        idx_free = [None, None]
        ivf_free = [None, None]
        xgT_free = [None, None]
        w1_free = [None, None]
        w2_free = [None, None]
        ysc_free = {}
        fys_chain = 0
        for e in range(E8):
            for tile in range(3):
                g3 = 3 * e + tile
                sl = g3 % 2
                base = ESTRIDE * e + 512 * tile
                if idx_free[sl] is not None:
                    K.waitp("vector", idx_free[sl])
                K.waitp("vector", xgT_free[sl])
                v_ic = None
                last_ga = None
                xg_free = getattr(K, "_xg_free", [None, None])
                for c in range(4):
                    csl = c % 2
                    K.waitp("sync", ivf_free[csl])
                    K.wait("sync", "fix", fix_total)
                    d_p = K.dma("sync", idxf[csl][:],
                                idxP_h[base + 128 * c:base + 128 * c + 128, :],
                                fam="fiv")
                    K.waitp("vector", d_p)
                    K.do("vector", lambda E, _sl=sl, _c=c, _cs=csl: E.tensor_copy(
                        idxi[_sl][:, _c:_c + 1], idxf[_cs][:, 0:1]))
                    v_ic = K.do_drained("vector", lambda E, _sl=sl, _c=c, _cs=csl:
                                        E.tensor_copy(pcolb[_sl][:, _c:_c + 1],
                                                      idxf[_cs][:, 1:2]),
                                        inc="sve")
                    ivf_free[csl] = ("sve", v_ic)
                    K.wait("gpsimd", "sve", v_ic)
                    K.waitp("gpsimd", xg_free[csl])
                    d_g = K.dma_indirect_in(xg[csl][:], xtok_h,
                                            idxi[sl][:, c:c + 1], fam="fga")
                    last_ga = d_g
                    K.waitp("tensor", d_g)
                    K.use_bank("tensor", 2)
                    K.use_bank("tensor", 3)
                    p_tg = None
                    for j in range(NJ):
                        p_tg = K.do("tensor", lambda E, _j=j, _cs=csl: E.transpose(
                            ps[:, 1024 + 128 * _j:1024 + 128 * _j + 128],
                            xg[_cs][:, 128 * _j:128 * _j + 128], ident[:]),
                            inc="spe" if j == NJ - 1 else None)
                    xg_free[csl] = ("spe", p_tg)
                    K._xg_free = xg_free
                    K.wait("vector", "spe", p_tg)
                    v_xgt = K.do("vector", lambda E, _sl=sl, _c=c: E.tensor_copy(
                        xgT3[_sl][:, :, 128 * _c:128 * _c + 128], ps3[:, 8:16, :]),
                        inc="sve")
                    K.release_bank(2, "sve", v_xgt)
                    K.release_bank(3, "sve", v_xgt)
                v_xgt = K.do("vector", lambda E: E.drain(), inc="sve")
                gelu_v = {}
                w1_dma = None
                last_w1 = None
                for f in range(NF):
                    q4 = f // 4
                    wsl = q4 % 2
                    if f % 4 == 0:
                        K.waitp("sync", w1_free[wsl])
                        for _f, _v in conv_all:
                            K.wait("sync", _f, _v)
                        w1_dma = K.dma("sync",
                                       w1bf[wsl][:].rearrange("p (fc j f) -> p fc j f",
                                                              fc=4, j=NJ),
                                       w1b_h[e, 4 * q4:4 * q4 + 4]
                                       .rearrange("fc p j f -> p fc j f"), fam="fw1")
                    bh = f % 2
                    K.use_bank("tensor", bh)
                    K.waitp("tensor", w1_dma)
                    K.wait("tensor", "sve", v_xgt)
                    p_h = None
                    for j in range(NJ):
                        p_h = K.do("tensor", lambda E, _f=f, _j=j, _sl=sl, _w=wsl,
                                   _bh=bh: E.matmul(
                            ps[:, 512 * _bh:512 * _bh + 512],
                            w1bf[_w][:, ((_f % 4) * 8 + _j) * 128:
                                     ((_f % 4) * 8 + _j) * 128 + 128],
                            xgT3[_sl][:, _j, :],
                            start=(_j == 0), stop=(_j == NJ - 1)),
                            inc="spe" if j == NJ - 1 else None)
                    last_w1 = p_h
                    if f % 4 == 3:
                        w1_free[wsl] = ("spe", p_h)
                    K.wait("scalar", "spe", p_h)
                    if f % 4 == 3:
                        K.do("scalar", lambda E, _f=f, _bh=bh, _e=e: E.activation(
                            h_sb[:, 512 * _f:512 * _f + 512],
                            ps[:, 512 * _bh:512 * _bh + 512],
                            AF.Gelu, bias=b1c[:, 8 * _f + _e:8 * _f + _e + 1]))
                        s_g = K.do("scalar", lambda E: E.drain(), inc="sse")
                    else:
                        s_g = K.do("scalar", lambda E, _f=f, _bh=bh, _e=e: E.activation(
                            h_sb[:, 512 * _f:512 * _f + 512],
                            ps[:, 512 * _bh:512 * _bh + 512],
                            AF.Gelu, bias=b1c[:, 8 * _f + _e:8 * _f + _e + 1]), inc="sse")
                    gelu_v[f] = s_g
                    K.release_bank(bh, "sse", s_g)
                xgT_free[sl] = ("spe", last_w1)
                for dh in range(2):
                    for b in range(4, 8):
                        K.use_bank("tensor", b)
                    p_w2 = None
                    for fq in range(NJ):
                        wsl2 = fq % 2
                        K.waitp("sync", w2_free[wsl2])
                        for _f, _v in conv_all:
                            K.wait("sync", _f, _v)
                        d_w2 = K.dma("sync",
                                     w2bf[wsl2][:].rearrange("p (fi d) -> p fi d", d=512),
                                     w2b_h[e, dh, 4 * fq:4 * fq + 4]
                                     .rearrange("fi p d -> p fi d"), fam="fw2")
                        K.waitp("tensor", d_w2)
                        K.wait("tensor", "sse", gelu_v[4 * fq + 3])
                        for f4 in range(4):
                            f = 4 * fq + f4
                            for sc in range(4):
                                p_w2 = K.do("tensor", lambda E, _f=f, _f4=f4, _sc=sc,
                                            _w=wsl2: E.matmul(
                                    ps[:, 512 * (4 + _sc):512 * (4 + _sc) + 512],
                                    h_sb[:, 512 * _f + 128 * _sc:
                                         512 * _f + 128 * _sc + 128],
                                    w2bf[_w][:, 512 * _f4:512 * _f4 + 512],
                                    start=(_f == 0), stop=(_f == NF - 1)),
                                    inc="spe" if (f4 == 3 and sc == 3) else None)
                        w2_free[wsl2] = ("spe", p_w2)
                    ysc = ysc2[sl]
                    for sc in range(4):
                        K.wait("vector", "spe", p_w2)
                        K.waitp("vector", ysc_free.get((sl, sc)))
                        if dh == 1:
                            v_ys = K.do_drained("vector", lambda E, _sc=sc, _dh=dh, _sl=sl:
                                                E.tensor_scalar(
                                ysc2[_sl][:, 1024 * _sc + 512 * _dh:
                                          1024 * _sc + 512 * _dh + 512],
                                ps[:, 512 * (4 + _sc):512 * (4 + _sc) + 512],
                                pcolb[_sl][:, _sc:_sc + 1], None, ALU.mult), inc="sve")
                        else:
                            v_ys = K.do("vector", lambda E, _sc=sc, _dh=dh, _sl=sl:
                                        E.tensor_scalar(
                                ysc2[_sl][:, 1024 * _sc + 512 * _dh:
                                          1024 * _sc + 512 * _dh + 512],
                                ps[:, 512 * (4 + _sc):512 * (4 + _sc) + 512],
                                pcolb[_sl][:, _sc:_sc + 1], None, ALU.mult), inc="sve")
                        K.release_bank(4 + sc, "sve", v_ys)
                        if dh == 1:
                            K.wait("gpsimd", "sve", v_ys)
                            K.wait("gpsimd", "fys", fys_chain)
                            d_ys = K.dma_indirect_out(y_h, idxi[sl][:, sc:sc + 1],
                                                      ysc2[sl][:, 1024 * sc:1024 * sc + 1024],
                                                      fam="fys", op=ALU.add)
                            ysc_free[(sl, sc)] = d_ys
                idx_free[sl] = ysc_free.get((sl, 3))
            fys_chain = K.cnt["fys"]
        K.flush()
        b_st.close()
        r2_st.close()
    else:
        y_sb = sb("y_sb", [128, NT * Dm], BF16)
        b_st = ExitStack()
        bsb = lambda n, sh, dt: b_st.enter_context(nc.sbuf_tensor(n, sh, dt))
        h_sb = bsb("h_sb", [128, NF * TTB], BF16)
        xnt_t = [bsb(f"xnt_t{i}", [128, NJ * TTB], BF16) for i in range(2)]
        w1bf = [bsb(f"w1bf{i}", [128, 4096], BF16) for i in range(2)]
        w2bf = [bsb(f"w2bf{i}", [128, 2048], BF16) for i in range(2)]

        K.wait("tensor", "sve", pn_last)
        K.wait("tensor", "sve", v_b2)
        for tc in range(NT):
            bb = 2 * (tc % 2)
            K.use_bank("tensor", bb)
            K.use_bank("tensor", bb + 1)
            p_y0 = None
            for ih in range(2):
                p_y0 = K.do("tensor", lambda E, _tc=tc, _ih=ih, _bb=bb: E.matmul(
                    ps[:, 512 * (_bb + _ih):512 * (_bb + _ih) + 512],
                    PnT[0:8, 128 * _tc:128 * _tc + 128],
                    b2bf[0:8, 512 * _ih:512 * _ih + 512],
                    start=True, stop=True), inc="spe" if ih == 1 else None)
            K.wait("vector", "spe", p_y0)
            v_y0 = K.do("vector", lambda E, _tc=tc, _bb=bb: E.tensor_copy(
                y_sb[:, 1024 * _tc:1024 * _tc + 1024],
                ps[:, 512 * _bb:512 * _bb + 1024]), inc="sve")
            K.release_bank(bb, "sve", v_y0)
            K.release_bank(bb + 1, "sve", v_y0)

        if upto == "YINIT":
            K.final_wait(); K.flush(); K.simulate(); st.close()
            return nc
        xnt_free = [None, None]
        w1_free = [None, None]
        w2_free = [None, None]
        for e in range(b_experts):
            for tt in range(NTT):
                sl = tt % 2
                K.waitp("sync", xnt_free[sl])
                d_xn = None
                for j in range(NJ):
                    d_xn = K.dma("sync", xnt_t[sl][:, 1024 * j:1024 * j + 1024],
                                 xnt_h[128 * j:128 * j + 128, 1024 * tt:1024 * tt + 1024],
                                 fam="fxn")
                last_w1_mm = None
                gelu_v = {}
                w1_dma = None
                for f in range(NF):
                    q4 = f // 4
                    wsl = q4 % 2
                    if f % 4 == 0:
                        K.waitp("sync", w1_free[wsl])
                        for _f, _v in conv_all:
                            K.wait("sync", _f, _v)
                        w1_dma = K.dma("sync",
                                       w1bf[wsl][:].rearrange("p (fc j f) -> p fc j f",
                                                              fc=4, j=NJ),
                                       w1b_h[e, 4 * q4:4 * q4 + 4]
                                       .rearrange("fc p j f -> p fc j f"), fam="fw1")
                    bh = f % 2
                    K.use_bank("tensor", 2 * bh)
                    K.use_bank("tensor", 2 * bh + 1)
                    K.waitp("tensor", w1_dma)
                    K.waitp("tensor", d_xn)
                    p_h = None
                    for half in range(2):
                        for j in range(NJ):
                            p_h = K.do("tensor", lambda E, _f=f, _h=half, _j=j, _sl=sl,
                                       _w=wsl, _bh=bh: E.matmul(
                                ps[:, 1024 * _bh + 512 * _h:1024 * _bh + 512 * _h + 512],
                                w1bf[_w][:, ((_f % 4) * 8 + _j) * 128:
                                         ((_f % 4) * 8 + _j) * 128 + 128],
                                xnt_t[_sl][:, 1024 * _j + 512 * _h:1024 * _j + 512 * _h + 512],
                                start=(_j == 0), stop=(_j == NJ - 1)),
                                inc="spe" if (half == 1 and j == NJ - 1) else None)
                    last_w1_mm = p_h
                    if f % 4 == 3:
                        w1_free[wsl] = ("spe", p_h)
                    K.wait("scalar", "spe", p_h)
                    K.do("scalar", lambda E, _f=f, _bh=bh, _e=e: E.activation(
                        h_sb[:, 1024 * _f:1024 * _f + 1024],
                        ps[:, 1024 * _bh:1024 * _bh + 1024],
                        AF.Gelu, bias=b1c[:, 8 * _f + _e:8 * _f + _e + 1]))
                    s_g = K.do("scalar", lambda E: E.drain(), inc="sse")
                    gelu_v[f] = s_g
                    K.release_bank(2 * bh, "sse", s_g)
                    K.release_bank(2 * bh + 1, "sse", s_g)
                xnt_free[sl] = ("spe", last_w1_mm)
                for dh in range(2):
                    for b in range(8):
                        K.use_bank("tensor", b)
                    last_mm = {}
                    for fq in range(NJ):
                        wsl2 = fq % 2
                        K.waitp("sync", w2_free[wsl2])
                        for _f, _v in conv_all:
                            K.wait("sync", _f, _v)
                        d_w2 = K.dma("sync",
                                     w2bf[wsl2][:].rearrange("p (fi d) -> p fi d", d=512),
                                     w2b_h[e, dh, 4 * fq:4 * fq + 4]
                                     .rearrange("fi p d -> p fi d"), fam="fw2")
                        K.waitp("tensor", d_w2)
                        K.wait("tensor", "sse", gelu_v[4 * fq + 3])
                        p_w2 = None
                        for f4 in range(4):
                            f = 4 * fq + f4
                            for tcq in range(8):
                                p_w2 = K.do("tensor", lambda E, _f=f, _f4=f4, _tc=tcq,
                                            _w=wsl2: E.matmul(
                                    ps[:, 512 * _tc:512 * _tc + 512],
                                    h_sb[:, 1024 * _f + 128 * _tc:
                                         1024 * _f + 128 * _tc + 128],
                                    w2bf[_w][:, 512 * _f4:512 * _f4 + 512],
                                    start=(_f == 0), stop=(_f == NF - 1)),
                                    inc="spe" if (f4 == 3 and tcq == 7) else None)
                        w2_free[wsl2] = ("spe", p_w2)
                        if fq == NJ - 1:
                            for tcq in range(8):
                                last_mm[tcq] = p_w2
                    for tcq in range(8):
                        tcg = 8 * tt + tcq
                        K.wait("vector", "spe", last_mm[tcq])
                        v_y = K.do("vector", lambda E, _tc=tcq, _tcg=tcg, _dh=dh, _e=e:
                                   E.scalar_tensor_tensor(
                            y_sb[:, 1024 * _tcg + 512 * _dh:1024 * _tcg + 512 * _dh + 512],
                            ps[:, 512 * _tc:512 * _tc + 512],
                            Pn[:, 8 * _tcg + _e:8 * _tcg + _e + 1],
                            y_sb[:, 1024 * _tcg + 512 * _dh:1024 * _tcg + 512 * _dh + 512],
                            op0=ALU.mult, op1=ALU.add), inc="sve")
                        K.release_bank(tcq, "sve", v_y)
        if debug:
            K.wait("gpsimd", "sve", K.cnt["sve"])
            K.dma("gpsimd", dbg_y, y_sb[:], fam="fo")
            K.wait("gpsimd", "sse", K.cnt["sse"])
            K.dma("gpsimd", dbg_h, h_sb[:], fam="fo")
        K.flush()
        b_st.close()

        if upto == "B":
            K.final_wait(); K.flush(); K.simulate(); st.close()
            return nc

    K.barrier()
    de_st = ExitStack()
    dsb = lambda n, sh, dt: de_st.enter_context(nc.sbuf_tensor(n, sh, dt))
    woT = dsb("woT", [128, NJ * Dm], BF16)
    woT3 = woT[:, :].rearrange("p (j n) -> p j n", n=Dm)
    att_t = [dsb(f"att_t{i}", [128, NJ * 512], BF16) for i in range(2)]
    s_t = [dsb(f"s_t{i}", [128, Dm], BF16) for i in range(2)]
    g_t = [dsb(f"g_t{i}", [128, Dm], F32) for i in range(2)]
    xe = [dsb(f"xe{i}", [128, Dm], F32) for i in range(2)]
    o_t = [dsb(f"o_t{i}", [128, Dm], F32) for i in range(2)]
    stgD = [dsb(f"stgD{i}", [128, Dm], F32) for i in range(2)]
    ysil = [dsb(f"ysil{i}", [128, Dm], F32) for i in range(2)] if routed else None
    ysil_free = [None, None]

    stgD_free = [None, None]
    for r in range(NJ):
        sl = r % 2
        K.waitp("sync", stgD_free[sl])
        d_w = K.dma("sync", stgD[sl][:], wo_h[128 * r:128 * r + 128, :], fam="fs")
        K.waitp("tensor", d_w)
        for b in (2 * sl, 2 * sl + 1):
            K.use_bank("tensor", b)
        p_t = None
        for j in range(NJ):
            p_t = K.do("tensor", lambda E, _j=j, _sl=sl: E.transpose(
                ps[:, 1024 * _sl + 128 * _j:1024 * _sl + 128 * _j + 128],
                stgD[_sl][:, 128 * _j:128 * _j + 128], ident[:]),
                inc="spe" if j == NJ - 1 else None)
        stgD_free[sl] = ("spe", p_t)
        K.wait("vector", "spe", p_t)
        v_e = K.do("vector", lambda E, _r=r, _sl=sl: E.tensor_copy(
            woT3[:, :, 128 * _r:128 * _r + 128], ps3[:, 8 * _sl:8 * _sl + 8, :]),
            inc="sve")
        for b in (2 * sl, 2 * sl + 1):
            K.release_bank(b, "sve", v_e)

    K.wait("tensor", "sve", v_e)
    att_free = [None, None]
    xe_free = [None, None]
    g_free = [None, None]
    o_free = [None, None]
    for qt in range(NJ):
        sl = qt % 2
        K.waitp("sync", att_free[sl])
        d_a = None
        for c in range(NJ):
            d_a = K.dma("sync", att_t[sl][:, 512 * c:512 * c + 512],
                        xat_h[128 * c:128 * c + 128, 512 * qt:512 * qt + 512],
                        fam="fat")
        last_de_mm = None
        for tc4 in range(4):
            tcg = 4 * qt + tc4
            tsl = tc4 % 2
            bb = 2 * tsl
            p_hi = None
            for ih in range(2):
                b = bb + ih
                K.use_bank("tensor", b)
                K.waitp("tensor", d_a)
                for c in range(NJ):
                    p_hi = K.do("tensor", lambda E, _c=c, _b=b, _ih=ih, _sl=sl,
                                _tc4=tc4: E.matmul(
                        ps[:, 512 * _b:512 * _b + 512],
                        att_t[_sl][:, 512 * _c + 128 * _tc4:512 * _c + 128 * _tc4 + 128],
                        woT3[:, _c, 512 * _ih:512 * _ih + 512],
                        start=(_c == 0), stop=(_c == NJ - 1)),
                        inc="spe" if c == NJ - 1 else None) or p_hi
            last_de_mm = p_hi
            K.wait("vector", "spe", p_hi)
            for ih in range(2):
                v_s = K.do("vector", lambda E, _b=bb + ih, _ih=ih, _tsl=tsl:
                           E.tensor_tensor(
                    s_t[_tsl][:, 512 * _ih:512 * _ih + 512],
                    ps[:, 512 * _b:512 * _b + 512],
                    thro[:, 512 * _ih:512 * _ih + 512], ALU.is_ge), inc="sve")
                K.release_bank(bb + ih, "sve", v_s)
            K.waitp("scalar", g_free[tsl])
            if routed:
                K.waitp("sync", ysil_free[tsl])
                d_ys = K.dma("sync", ysil[tsl][:], y_h[128 * tcg:128 * tcg + 128, :],
                             fam="fy")
                K.waitp("scalar", d_ys)
                s_si = K.do_drained("scalar", lambda E, _tsl=tsl: E.activation(
                    g_t[_tsl][:], ysil[_tsl][:], AF.Silu), inc="sse")
                ysil_free[tsl] = ("sse", s_si)
            else:
                s_si = K.do_drained("scalar", lambda E, _tcg=tcg, _tsl=tsl: E.activation(
                    g_t[_tsl][:], y_sb[:, 1024 * _tcg:1024 * _tcg + 1024], AF.Silu),
                    inc="sse")
            K.waitp("sync", xe_free[tsl])
            d_x = K.dma("sync", xe[tsl][:], x_h[128 * tcg:128 * tcg + 128, :], fam="fx")
            K.wait("vector", "sse", s_si)
            K.waitp("vector", d_x)
            K.waitp("vector", o_free[tsl])
            K.drain("vector")
            K.do("vector", lambda E, _tsl=tsl: E.tensor_tensor(
                g_t[_tsl][:], g_t[_tsl][:], s_t[_tsl][:], ALU.mult))
            K.drain("vector")
            v_cmb = K.do_drained("vector", lambda E, _tsl=tsl: E.tensor_tensor(
                o_t[_tsl][:], g_t[_tsl][:], xe[_tsl][:], ALU.add), inc="sve")
            g_free[tsl] = ("sve", v_cmb)
            xe_free[tsl] = ("sve", v_cmb)
            K.wait("gpsimd", "sve", v_cmb)
            out_wr = K.dma("gpsimd", out_h[128 * tcg:128 * tcg + 128, :], o_t[tsl][:], fam="fo")
            o_free[tsl] = out_wr
            if debug:
                K.dma("gpsimd", dbg_s[128 * tcg:128 * tcg + 128, :], s_t[tsl][:], fam="fo")
        att_free[sl] = ("spe", last_de_mm)
    K.final_wait()
    K.flush()
    de_st.close()
    K.simulate()
    st.close()
    return nc


_CACHED = {}


def _get_program(upto="DE", debug=False):
    key = (upto, debug)
    if key not in _CACHED:
        _CACHED[key] = build_program(upto, debug)
    return _CACHED[key]


def make_in_maps(inputs):
    f = lambda k: np.ascontiguousarray(np.asarray(inputs[k], dtype=np.float32))
    x = f("L0_attn_x")
    base = {
        "wq": f("L0_attn_Wq"), "wk": f("L0_attn_Wk"), "wv": f("L0_attn_Wv"),
        "wo": f("L0_attn_Wo"),
        "bq": f("L0_attn_bq").reshape(8, 128), "bk": f("L0_attn_bk").reshape(8, 128),
        "bv": f("L0_attn_bv").reshape(1, 1024), "bo": f("L0_attn_bo").reshape(1, 1024),
        "wg": f("L0_ffn_Wg"), "w1": f("L0_ffn_W1"), "b1": f("L0_ffn_b1"),
        "w2": f("L0_ffn_W2"), "b2": f("L0_ffn_b2"),
    }
    return [dict(base, x=np.ascontiguousarray(x[i])) for i in range(8)]


def kernel(**inputs):
    nc = _get_program()
    in_maps = make_in_maps(inputs)
    res = run_bass_kernel_spmd(nc, in_maps, core_ids=list(range(8)))
    return np.stack([np.asarray(res.results[i]["out"]) for i in range(8)], axis=0)
